# version 02
# baseline (speedup 1.0000x reference)
import os
import numpy as np
import ml_dtypes
import concourse.bass as bass
import concourse.mybir as mybir
from concourse.bass_utils import run_bass_kernel_spmd
from contextlib import ExitStack

F32 = mybir.dt.float32
F16 = mybir.dt.float16
BF16 = mybir.dt.bfloat16
I8 = mybir.dt.int8
AF = mybir.ActivationFunctionType
ALU = mybir.AluOpType
AX = mybir.AxisListType

B, N, D, H, HD = 4, 4096, 2048, 16, 128
TPC = 2048
NTC = 16
DI = 8192
KB1 = 16
KB2 = 64
GTC = 4
NG = NTC // GTC
GA = 2
EPS = 1e-5

STAGE = int(os.environ.get("MK_STAGE", "3"))
SUB = int(os.environ.get("MK_SUB", "9"))
SDBG = bool(os.environ.get("MK_SDBG"))
QG = int(os.environ.get("MK_QG", "4"))
GBAR = int(os.environ.get("MK_GBAR", "0"))

ENGINES = ("sync", "tensor", "vector", "scalar", "gpsimd")


class Res:
    __slots__ = ("name", "w", "rs")

    def __init__(self, name):
        self.name = name
        self.w = None
        self.rs = []


class KB:
    DMA_LANES = {"sync": 4, "gpsimd": 2, "scalar": 2}

    def __init__(self, nc, ctx):
        self.nc = nc
        self.q = {e: [] for e in ENGINES}
        self.sem = {}
        self.dsems = set()
        for name in ("pT", "pV", "pA", "pG", "cc"):
            h = ctx.enter_context(nc.semaphore(name))
            self.sem[name] = [h, 0]
        self.qpre = {"sync": "dS", "gpsimd": "dG", "scalar": "dC"}
        for qn, nl in self.DMA_LANES.items():
            for l in range(nl):
                name = f"{self.qpre[qn]}{l}"
                h = ctx.enter_context(nc.semaphore(name))
                self.sem[name] = [h, 0]
                self.dsems.add(name)
        self.esem = {"tensor": "pT", "vector": "pV", "scalar": "pA",
                     "gpsimd": "pG"}
        self.waited = {e: {} for e in ENGINES}
        self.nops = 0

    def _wait(self, eng, sname, val):
        if self.waited[eng].get(sname, -1) >= val:
            return
        if os.environ.get("MK_DBG2") and (eng, sname, val) == tuple(
                os.environ["MK_DBG2"].split(",", 2)[:2]) + (int(os.environ["MK_DBG2"].split(",")[2]),):
            import traceback
            traceback.print_stack()
        assert val <= self.sem[sname][1], \
            f"{eng} waits {sname}>={val} > planned {self.sem[sname][1]}"
        h = self.sem[sname][0]
        self.q[eng].append(lambda q, h=h, v=val: q.wait_ge(h, v))
        self.waited[eng][sname] = val

    def _deps(self, eng, reads, writes):
        for r in reads:
            if r.w is not None:
                self._wait(eng, r.w[0], r.w[1])
        for w in writes:
            if w.w is not None:
                self._wait(eng, w.w[0], w.w[1])
            for (sn, v) in w.rs:
                self._wait(eng, sn, v)

    def _record(self, sname, val, reads, writes):
        for r in reads:
            r.rs.append((sname, val))
        for w in writes:
            w.w = (sname, val)
            w.rs = []

    def eng(self, eng, fn, reads=(), writes=()):
        self._deps(eng, reads, writes)
        sname = self.esem[eng]
        s = self.sem[sname]
        s[1] += 1
        v = s[1]
        h = s[0]

        if os.environ.get("MK_DBG"):
            import traceback
            tb = traceback.extract_stack()[-3]

            def w(q, fn=fn, h=h, eng=eng, v=v, tb=tb):
                before = sum(len(b.instructions) for b in self.nc.m.functions[0].blocks)
                fn(q).then_inc(h, 1)
                after = sum(len(b.instructions) for b in self.nc.m.functions[0].blocks)
                if after - before != 1:
                    print(f"SPLIT OP: {eng} v={v} n={after-before} at {tb.filename}:{tb.lineno}")
        else:
            def w(q, fn=fn, h=h):
                fn(q).then_inc(h, 1)
        self.q[eng].append(w)
        self.nops += 1
        self._record(sname, v, reads, writes)
        return v

    def tgroup(self, fns, reads=(), writes=()):
        self._deps("tensor", reads, writes)
        s = self.sem["pT"]
        s[1] += 1
        v = s[1]
        h = s[0]
        for i, fn in enumerate(fns):
            if i == len(fns) - 1:
                def w(q, fn=fn, h=h):
                    fn(q).then_inc(h, 1)
                self.q["tensor"].append(w)
            else:
                self.q["tensor"].append(fn)
        self.nops += len(fns)
        self._record("pT", v, reads, writes)
        return v

    def dma(self, qname, out, in_, reads=(), writes=(), slow=False, lane=0):
        self._deps(qname, reads, writes)
        sname = f"{self.qpre[qname]}{lane % self.DMA_LANES[qname]}"
        s = self.sem[sname]
        self._wait(qname, sname, s[1])
        s[1] += 16
        v = s[1]
        h = s[0]
        self.q[qname].append(
            lambda q, o=out, i=in_, h=h, sl=slow: q.dma_start(
                out=o, in_=i, allow_slow_non_contiguous=sl).then_inc(h, 16))
        self.nops += 1
        self._record(sname, v, reads, writes)
        return v

    def collective(self, kind, op, groups, ins, outs, reads=(), writes=()):
        self._deps("gpsimd", reads, writes)
        s = self.sem["cc"]
        s[1] += 1
        v = s[1]
        h = s[0]

        def w(q):
            q.collective_compute(kind, op, replica_groups=groups,
                                 ins=ins, outs=outs).then_inc(h)
        self.q["gpsimd"].append(w)
        self._record("cc", v, reads, writes)
        return v

    def barrier(self):
        vals = {}
        for e in ("tensor", "vector", "scalar", "gpsimd"):
            sn = self.esem[e]
            s = self.sem[sn]
            self.q[e].append(lambda q, h=s[0], v=s[1]: q.wait_ge(h, v))
            s[1] += 1
            vals[sn] = s[1]
            h = s[0]
            self.q[e].append(lambda q, h=h: q.sem_inc(h, 1))
        for sn in self.dsems:
            vals[sn] = self.sem[sn][1]
        for e in ENGINES:
            for sn, v in vals.items():
                if sn == self.esem.get(e) or v == 0:
                    continue
                if self.waited[e].get(sn, -1) >= v:
                    continue
                h = self.sem[sn][0]
                self.q[e].append(lambda q, h=h, v=v: q.wait_ge(h, v))
                self.waited[e][sn] = v

    def finalize(self, block):
        for e in ENGINES:
            fns = self.q[e]
            sec = getattr(block, e)

            def runner(q, fns=fns):
                for f in fns:
                    f(q)
            sec(runner)


class Arena:
    def __init__(self, ap, nbytes):
        self.ap = ap
        self.nbytes = nbytes
        self.off = 0
        self.marks = []
        self.hi = 0

    def alloc(self, rows, n, dt):
        per = n * mybir.dt.size(dt)
        off = (self.off + 31) & ~31
        assert off + per <= self.nbytes, \
            f"arena overflow: {off}+{per} > {self.nbytes}"
        self.off = off + per
        self.hi = max(self.hi, self.off)
        return self.ap[0:rows, off // 2:(off + per) // 2].bitcast(dt)

    def push(self):
        self.marks.append(self.off)

    def pop(self):
        self.off = self.marks.pop()


def build_nc():
    nc = bass.Bass("TRN2", target_bir_lowering=False)

    def din(n, sh, dt=F32):
        return nc.dram_tensor(n, sh, dt, kind="ExternalInput").ap()

    x_in = din("x", [TPC, D])
    ld_in = din("log_decay", [H])
    wq_in = din("Wq", [D, D])
    wk_in = din("Wk", [D, D])
    wv_in = din("Wv", [D, D])
    wo_in = din("Wo", [D, D])
    wg1_in = din("Wg1", [D, HD])
    wg2_in = din("Wg2", [HD, D])
    nw_in = din("norm_w", [D])
    w1_in = din("w1", [DI, D])
    b1_in = din("b1", [DI])
    lng_in = din("ln_g", [DI])
    lnb_in = din("ln_b", [DI])
    w2_in = din("w2", [D, DI])
    b2_in = din("b2", [D])
    mask_in = din("mask", [1, 1])
    eyef_in = din("eyef", [128, 128])
    eyeb_in = din("eyeb", [128, 128], BF16)
    onesr_in = din("onesr", [1, 128])
    onesb_in = din("onesb", [1, 128], BF16)
    onesc_in = din("onesc", [128, 128])
    diff_in = din("difft", [128, 128])
    pos1_in = din("pos1", [128, 1])
    posk_in = din("posk", [128, 1])

    out_t = nc.dram_tensor("out", [TPC, D], F32, kind="ExternalOutput").ap()

    w1qT = nc.dram_tensor("w1qT", [KB1, 128, DI], BF16).ap()
    w2qT = nc.dram_tensor("w2qT", [KB2, 128, D], BF16).ap()
    wqb = nc.dram_tensor("wqb", [KB1, 128, D], BF16).ap()
    wkb = nc.dram_tensor("wkb", [KB1, 128, D], BF16).ap()
    wvb = nc.dram_tensor("wvb", [KB1, 128, D], BF16).ap()
    wob = nc.dram_tensor("wob", [KB1, 128, D], BF16).ap()
    wg1b = nc.dram_tensor("wg1b", [KB1, 128, HD], BF16).ap()
    kscr = nc.dram_tensor("kscr", [NTC, 128, D], BF16).ap()
    vscr = nc.dram_tensor("vscr", [NTC, 128, D], BF16).ap()
    sdbg_t = nc.dram_tensor("sdbg", [NTC, 128, H * HD], F32).ap()
    odbg_t = nc.dram_tensor("odbg", [NTC, 128, D], F32).ap()
    ndbg_t = nc.dram_tensor("ndbg", [NTC, 128, D], F32).ap()
    gdbg_t = nc.dram_tensor("gdbg", [NTC, 128, D], F32).ap()
    cc_in_t = nc.dram_tensor("cc_in", [128, H * HD], F32)
    cc_out_t = nc.dram_tensor("cc_out", [2, 128, H * HD], F32)
    g_bf = nc.dram_tensor("g_bf", [DI], BF16).ap()
    b_bf = nc.dram_tensor("b_bf", [DI], BF16).ap()
    b1_bf = nc.dram_tensor("b1_bf", [DI], BF16).ap()
    b2h_bf = nc.dram_tensor("b2h_bf", [D], BF16).ap()
    uv1d = nc.dram_tensor("uv1d", [16, 128], BF16).ap()
    uv2d = nc.dram_tensor("uv2d", [16, 128], BF16).ap()

    with ExitStack() as ctx:
        kb = KB(nc, ctx)

        def sb(name, sh, dt):
            return ctx.enter_context(nc.sbuf_tensor(name, sh, dt))

        X = sb("X", [128, NTC * D], BF16)
        st_sum = sb("st_sum", [128, NTC], F32)
        st_sq = sb("st_sq", [128, NTC], F32)
        st_mx = sb("st_mx", [128, NTC], F32)
        st_mn = sb("st_mn", [128, NTC], F32)
        cA = sb("cA", [128, NTC], F32)
        cB = sb("cB", [128, NTC], F32)
        cUh = sb("cUh", [128, NTC], F32)
        u1f = sb("u1f", [128, NTC], F32)
        cMu = sb("cMu", [128, NTC], F32)
        cMb = sb("cMb", [128, NTC], F32)
        rows16 = sb("rows16", [16, 128], BF16)
        rows16f = sb("rows16f", [16, 128], F32)
        uv1r = sb("uv1r", [1, 16 * 128], BF16)
        uv2r = sb("uv2r", [1, 4 * 128], BF16)
        b1stg = sb("b1stg", [1, 2 * 512], BF16)
        b2stg = sb("b2stg", [1, 2 * 512], BF16)
        swc = sb("swc", [128, 2], F32)
        mwc = sb("mwc", [128, 2], F32)
        maskb = sb("maskb", [128, 1], F32)
        eyef = sb("s_eyef", [128, 128], F32)
        eyeb = sb("s_eyeb", [128, 128], BF16)
        onesr = sb("s_onesr", [1, 128], F32)
        onesb = sb("s_onesb", [1, 128], BF16)
        onesc = sb("s_onesc", [128, 128], F32)
        difft = sb("s_difft", [128, 128], F32)
        pos1 = sb("s_pos1", [128, 1], F32)
        posk = sb("s_posk", [128, 1], F32)
        ldbc = sb("ldbc", [128, H], F32)
        qdt = sb("qdt", [128, H], F32)
        kdt = sb("kdt", [128, H], F32)
        cdt = sb("cdt", [128, H], F32)
        tmpc = sb("tmpc", [128, 96], F32)
        gsum = sb("gsum", [128, 64], F32)
        gsq = sb("gsq", [128, 64], F32)
        am2 = sb("am2", [128, 16], F32)
        ms2 = sb("ms2", [128, 16], F32)
        c2A = sb("c2A", [128, GTC], F32)
        c2B = sb("c2B", [128, GTC], F32)
        cS2 = sb("cS2", [128, GTC], F32)
        cU2 = sb("cU2", [128, GTC], F32)
        nwcol = sb("nwcol", [128, KB1], F32)
        halfc = sb("halfc", [128, 1], F32)
        wsum = sb("wsum", [128, 64], F32)
        uvt = sb("uvt", [128, 16], F32)

        AR = sb("ARENA", [128, 66064], BF16)
        A_ = Arena(AR[:], 132128)

        pb = [ctx.enter_context(nc.psum_tensor(f"pb{i}", [128, 512], F32))
              for i in range(8)]
        pbb = [p[:].bitcast(BF16) for p in pb]

        RX = [Res(f"x{t}") for t in range(NTC)]
        RST = Res("stats")
        RSC = Res("scal")
        RPB = [Res(f"pb{i}") for i in range(8)]
        RROW = Res("rows")
        RWD = Res("wscr_dram")
        RS_ = [Res("S0"), Res("S1")]
        RDEC = Res("decays")
        RCC = Res("ccres")
        RTMP = Res("tmpcols")

        block = ctx.enter_context(nc.Block())

        def vop(fn, reads=(), writes=()):
            return kb.eng("vector", fn, reads, writes)

        def aop(fn, reads=(), writes=()):
            return kb.eng("scalar", fn, reads, writes)

        def gop(fn, reads=(), writes=()):
            return kb.eng("gpsimd", fn, reads, writes)

        def bcast_row(src_ap, dst_ap, n, bank, rsrc, rdst):
            lhs = onesr[0:1, :] if src_ap.dtype == F32 else onesb[0:1, :]
            kb.tgroup([lambda q, lhs=lhs, src=src_ap, n=n, bank=bank: q.matmul(
                out=pb[bank][:, 0:n], lhsT=lhs, rhs=src, start=True, stop=True)],
                reads=[rsrc, RROW], writes=[RPB[bank]])
            kb.eng("vector", lambda q, dst=dst_ap, n=n, bank=bank: q.tensor_copy(
                out=dst, in_=pb[bank][:, 0:n]),
                reads=[RPB[bank]], writes=[rdst])

        def cols_to_rows(cols_ap, ncol, dscratch, row_ap, rres, bank=6):
            kb.tgroup([lambda q: q.transpose(out=pb[bank][0:ncol, 0:128],
                                             in_=cols_ap, identity=eyef[:])],
                      reads=[rres, RROW], writes=[RPB[bank]])
            vop(lambda q: q.tensor_copy(out=rows16[0:ncol, :],
                                        in_=pb[bank][0:ncol, 0:128]),
                reads=[RPB[bank]], writes=[RROW])
            kb.dma("sync", dscratch[0:ncol, :], rows16[0:ncol, :],
                   reads=[RROW], writes=[RROW], lane=3)
            kb.dma("sync", row_ap,
                   dscratch.rearrange("a b -> (a b)")[0:ncol * 128]
                   .rearrange("(a b) -> a b", a=1),
                   reads=[RROW], writes=[RROW], lane=3)

        for dst, src in [(eyef[:], eyef_in), (eyeb[:], eyeb_in),
                         (onesr[:], onesr_in), (onesb[:], onesb_in),
                         (onesc[:], onesc_in), (difft[:], diff_in),
                         (pos1[:], pos1_in), (posk[:], posk_in)]:
            kb.dma("sync", dst, src, writes=[RROW], lane=3)
        kb.dma("sync", rows16f[:], nw_in.rearrange("(k p) -> k p", p=128),
               writes=[RROW], lane=3)
        kb.tgroup([lambda q: q.transpose(out=pb[6][:, 0:16],
                                         in_=rows16f[:], identity=eyef[0:16, 0:16])],
                  reads=[RROW], writes=[RPB[6]])
        kb.eng("vector", lambda q: q.tensor_copy(out=nwcol[:], in_=pb[6][:, 0:16]),
               reads=[RPB[6]], writes=[RROW])
        kb.dma("sync", ldbc[:],
               ld_in.rearrange("(a b) -> a b", a=1).broadcast_to([128, H]),
               writes=[RDEC], lane=3)
        kb.dma("sync", maskb[:], mask_in.broadcast_to([128, 1]), writes=[RROW], lane=3)

        A_.push()
        rowstg = A_.alloc(16, 512, F32)
        rowbf = A_.alloc(16, 512, BF16)
        RRS = Res("rowstg")
        RRB = Res("rowbf")
        for dst, src, nr in [(b1_bf, b1_in, 16), (g_bf, lng_in, 16),
                             (b_bf, lnb_in, 16)]:
            kb.dma("sync", rowstg[0:nr, :],
                   src.rearrange("(a b) -> a b", a=nr), reads=[RRS], writes=[RRS], lane=3)
            vop(lambda q, nr=nr: q.tensor_copy(out=rowbf[0:nr, :],
                                               in_=rowstg[0:nr, :]),
                reads=[RRS], writes=[RRB])
            kb.dma("sync", dst.rearrange("(a b) -> a b", a=nr), rowbf[0:nr, :],
                   reads=[RRB], writes=[RROW], lane=3)
        kb.dma("sync", rowstg[0:4, :], b2_in.rearrange("(a b) -> a b", a=4),
               reads=[RRS], writes=[RRS], lane=3)
        vop(lambda q: q.tensor_scalar(out=rowbf[0:4, :], in0=rowstg[0:4, :],
                                      scalar1=0.5, scalar2=None, op0=ALU.mult),
            reads=[RRS], writes=[RRB])
        kb.dma("sync", b2h_bf.rearrange("(a b) -> a b", a=4), rowbf[0:4, :],
               reads=[RRB], writes=[RROW], lane=3)

        vop(lambda q: q.tensor_tensor(
            out=tmpc[:, 0:H], in0=ldbc[:],
            in1=pos1[:, 0:1].broadcast_to([128, H]), op=ALU.mult),
            reads=[RDEC, RROW], writes=[RTMP])
        aop(lambda q: q.activation(out=qdt[:], in_=tmpc[:, 0:H], func=AF.Exp),
            reads=[RTMP], writes=[RDEC])
        vop(lambda q: q.tensor_tensor(
            out=tmpc[:, 0:H], in0=ldbc[:],
            in1=posk[:, 0:1].broadcast_to([128, H]), op=ALU.mult),
            reads=[RDEC, RROW, RTMP], writes=[RTMP])
        aop(lambda q: q.activation(out=kdt[:], in_=tmpc[:, 0:H], func=AF.Exp),
            reads=[RTMP], writes=[RDEC])
        aop(lambda q: q.activation(out=cdt[:], in_=ldbc[:], func=AF.Exp,
                                   scale=128.0),
            reads=[RDEC], writes=[RDEC])

        vop(lambda q: q.memset(halfc[:], 1536.0), writes=[RROW])

        xstg = A_.alloc(128, 2 * D, F32)
        RXS = [Res("xs0"), Res("xs1")]
        sqst = A_.alloc(128, 2 * D, BF16)
        RSQ2 = [Res("sq0"), Res("sq1")]
        for t in range(NTC):
            sl = xstg[:, (t % 2) * D:(t % 2 + 1) * D]
            kb.dma("sync", sl, x_in[t * 128:(t + 1) * 128, :],
                   reads=[RXS[t % 2]], writes=[RXS[t % 2]], lane=t % 2)
            vop(lambda q, sl=sl, t=t: q.tensor_reduce(
                out=st_sum[:, t:t + 1], in_=sl, axis=AX.X, op=ALU.add),
                reads=[RXS[t % 2]], writes=[RST])
            vop(lambda q, sl=sl, t=t: q.tensor_reduce(
                out=st_mx[:, t:t + 1], in_=sl, axis=AX.X, op=ALU.max),
                reads=[RXS[t % 2]], writes=[RST])
            vop(lambda q, sl=sl, t=t: q.tensor_reduce(
                out=st_mn[:, t:t + 1], in_=sl, axis=AX.X, op=ALU.min),
                reads=[RXS[t % 2]], writes=[RST])
            sq = sqst[:, (t % 2) * D:(t % 2 + 1) * D]
            aop(lambda q, sl=sl, sq=sq, t=t: q.activation(
                out=sq, in_=sl, func=AF.Square, accum_out=st_sq[:, t:t + 1]),
                reads=[RXS[t % 2]], writes=[RSQ2[t % 2], RST])
            gop(lambda q, sl=sl, t=t: q.tensor_copy(
                out=X[:, t * D:(t + 1) * D], in_=sl),
                reads=[RXS[t % 2]], writes=[RX[t]])

        wstg = A_.alloc(128, 2 * D, F32)
        RWS = [Res("ws0"), Res("ws1")]
        qstg = A_.alloc(128, D, F32)
        RQS = Res("qs")
        qstg2 = A_.alloc(128, D, F16)
        RQ2 = Res("qs2")
        qbs = A_.alloc(128, D, BF16)
        RQB = Res("qb")
        tstg = A_.alloc(128, 2 * D, BF16)
        RTS = [Res("ts0"), Res("ts1")]

        def w_scale(w_ap, rows, cols, col_i, dmaq):
            ntile = (rows // 128) * (cols // D)
            assert ntile == 64
            for i in range(ntile):
                r = i // (cols // D)
                c = i % (cols // D)
                sl = wstg[:, (i % 2) * D:(i % 2 + 1) * D]
                kb.dma(dmaq, sl, w_ap[r * 128:(r + 1) * 128, c * D:(c + 1) * D],
                       reads=[RWS[i % 2]], writes=[RWS[i % 2]], lane=i % 2)
                vop(lambda q, sl=sl, i=i: q.tensor_reduce(
                    out=wsum[:, i:i + 1], in_=sl, axis=AX.X, op=ALU.add,
                    apply_absolute_value=True),
                    reads=[RWS[i % 2]], writes=[RTMP])
            vop(lambda q: q.tensor_reduce(
                out=tmpc[:, 0:1], in_=wsum[:], axis=AX.X, op=ALU.add),
                reads=[RTMP], writes=[RTMP])
            kb.tgroup([lambda q: q.matmul(out=pb[7][:, 0:1], lhsT=onesc[:],
                                          rhs=tmpc[:, 0:1], start=True, stop=True)],
                      reads=[RTMP, RROW], writes=[RPB[7]])
            cnt = float(rows * cols)
            vop(lambda q, col_i=col_i, cnt=cnt: q.tensor_scalar(
                out=mwc[:, col_i:col_i + 1], in0=pb[7][:, 0:1],
                scalar1=1.0 / cnt, scalar2=1e-5, op0=ALU.mult, op1=ALU.max),
                reads=[RPB[7]], writes=[RSC])
            vop(lambda q, col_i=col_i: q.reciprocal(
                out=swc[:, col_i:col_i + 1], in_=mwc[:, col_i:col_i + 1]),
                reads=[RSC], writes=[RSC])

        def w_quant(w_ap, scratch, rows, cols, col_i, dmaq):
            scr_d = scratch.rearrange("k d o -> d k o")
            npc = cols // D
            for r in range(rows // 128):
                for c in range(npc):
                    i = r * npc + c
                    stf = wstg[:, (i % 2) * D:(i % 2 + 1) * D]
                    kb.dma(dmaq, stf,
                           w_ap[r * 128:(r + 1) * 128, c * D:(c + 1) * D],
                           reads=[RWS[i % 2]], writes=[RWS[i % 2]], lane=i % 2)
                    vop(lambda q, stf=stf, col_i=col_i: q.tensor_scalar(
                        out=qstg[:], in0=stf, scalar1=swc[:, col_i:col_i + 1],
                        scalar2=1.0, op0=ALU.mult, op1=ALU.min),
                        reads=[RWS[i % 2], RSC], writes=[RQS])
                    vop(lambda q: q.tensor_scalar(
                        out=qstg2[:], in0=qstg[:], scalar1=-1.0, scalar2=1536.0,
                        op0=ALU.max, op1=ALU.add),
                        reads=[RQS], writes=[RQ2])
                    gop(lambda q: q.tensor_scalar(
                        out=qbs[:], in0=qstg2[:], scalar1=-1536.0, scalar2=None,
                        op0=ALU.add),
                        reads=[RQ2], writes=[RQB])
                    ts = tstg[:, (i % 2) * D:(i % 2 + 1) * D]
                    for g in range(4):
                        bank = 4 + (g % 2)
                        fns = []
                        for j in range(4):
                            blk = g * 4 + j
                            fns.append(lambda q, blk=blk, j=j, bank=bank:
                                       q.transpose(out=pbb[bank][:, j * 128:(j + 1) * 128],
                                                   in_=qbs[:, blk * 128:(blk + 1) * 128],
                                                   identity=eyeb[:]))
                        kb.tgroup(fns, reads=[RQB, RROW], writes=[RPB[bank]])
                        aop(lambda q, g=g, bank=bank, ts=ts: q.copy(
                            out=ts[:, g * 512:(g + 1) * 512], in_=pbb[bank][:, 0:512]),
                            reads=[RPB[bank]], writes=[RTS[i % 2]])
                    kb.dma(dmaq,
                           scr_d[:, c * 16:(c + 1) * 16, r * 128:(r + 1) * 128],
                           ts.rearrange("p (k o) -> p k o", k=16),
                           reads=[RTS[i % 2]], writes=[RWD], lane=i % 2)

        def w_bf16_copy(w_ap, scratch, fold_nw, dmaq):
            cols = w_ap.shape[1]
            width = min(cols, D)
            for k in range(KB1):
                i = k
                stf = wstg[:, (i % 2) * D:(i % 2) * D + width]
                kb.dma(dmaq, stf, w_ap[k * 128:(k + 1) * 128, :],
                       reads=[RWS[i % 2]], writes=[RWS[i % 2]], lane=i % 2)
                tb = tstg[:, (i % 2) * D:(i % 2) * D + width]
                if fold_nw:
                    vop(lambda q, stf=stf, tb=tb, k=k: q.tensor_scalar(
                        out=tb, in0=stf, scalar1=nwcol[:, k:k + 1],
                        scalar2=None, op0=ALU.mult),
                        reads=[RWS[i % 2], RROW], writes=[RTS[i % 2]])
                else:
                    vop(lambda q, stf=stf, tb=tb: q.tensor_copy(out=tb, in_=stf),
                        reads=[RWS[i % 2]], writes=[RTS[i % 2]])
                kb.dma(dmaq, scratch[k, :, :], tb,
                       reads=[RTS[i % 2]], writes=[RWD], lane=i % 2)

        w_scale(w1_in, DI, D, 0, "sync")
        w_scale(w2_in, D, DI, 1, "scalar")
        w_quant(w1_in, w1qT, DI, D, 0, "sync")
        w_quant(w2_in, w2qT, D, DI, 1, "sync")
        w_bf16_copy(wq_in, wqb, False, "scalar")
        w_bf16_copy(wk_in, wkb, False, "scalar")
        w_bf16_copy(wv_in, wvb, False, "scalar")
        w_bf16_copy(wo_in, wob, True, "scalar")
        w_bf16_copy(wg1_in, wg1b, False, "scalar")
        A_.pop()

        kb.barrier()

        def tc16(i):
            return tmpc[:, i * 16:(i + 1) * 16]

        def ffn_scalar_prep():
            vop(lambda q: q.tensor_scalar(out=tc16(0), in0=st_sum[:],
                                          scalar1=1.0 / D, scalar2=None, op0=ALU.mult),
                reads=[RST], writes=[RSC])
            vop(lambda q: q.tensor_scalar(out=tc16(1), in0=st_sq[:],
                                          scalar1=1.0 / D, scalar2=None, op0=ALU.mult),
                reads=[RST], writes=[RSC])
            vop(lambda q: q.tensor_tensor(out=tc16(2), in0=tc16(0), in1=tc16(0),
                                          op=ALU.mult),
                reads=[RSC], writes=[RSC])
            vop(lambda q: q.tensor_tensor(out=tc16(3), in0=tc16(1), in1=tc16(2),
                                          op=ALU.subtract),
                reads=[RSC], writes=[RSC])
            vop(lambda q: q.tensor_scalar(out=tc16(2), in0=tc16(3),
                                          scalar1=1.0 + EPS, scalar2=EPS * EPS,
                                          op0=ALU.mult, op1=ALU.add),
                reads=[RSC], writes=[RSC])
            vop(lambda q: q.reciprocal(out=tc16(4), in_=tc16(2)),
                reads=[RSC], writes=[RSC])
            aop(lambda q: q.activation(out=tc16(5), in_=tc16(4), func=AF.Sqrt),
                reads=[RSC], writes=[RSC])
            vop(lambda q: q.tensor_tensor(out=tc16(2), in0=st_mx[:], in1=tc16(0),
                                          op=ALU.subtract),
                reads=[RST, RSC], writes=[RSC])
            vop(lambda q: q.tensor_tensor(out=tc16(1), in0=tc16(0), in1=st_mn[:],
                                          op=ALU.subtract),
                reads=[RST, RSC], writes=[RSC])
            vop(lambda q: q.tensor_tensor(out=tc16(3), in0=tc16(2), in1=tc16(1),
                                          op=ALU.max),
                reads=[RSC], writes=[RSC])
            vop(lambda q: q.tensor_tensor(out=tc16(2), in0=tc16(5), in1=tc16(3),
                                          op=ALU.mult),
                reads=[RSC], writes=[RSC])
            vop(lambda q: q.tensor_scalar(out=tc16(1), in0=tc16(2), scalar1=1e-5,
                                          scalar2=None, op0=ALU.max),
                reads=[RSC], writes=[RSC])
            vop(lambda q: q.reciprocal(out=tc16(2), in_=tc16(1)),
                reads=[RSC], writes=[RSC])
            vop(lambda q: q.tensor_scalar(out=tc16(4), in0=tc16(2), scalar1=127.0,
                                          scalar2=None, op0=ALU.mult),
                reads=[RSC], writes=[RSC])
            vop(lambda q: q.tensor_tensor(out=cA[:], in0=tc16(5), in1=tc16(4),
                                          op=ALU.mult),
                reads=[RSC], writes=[RSC])
            vop(lambda q: q.tensor_tensor(out=tc16(2), in0=tc16(0), in1=cA[:],
                                          op=ALU.mult),
                reads=[RSC], writes=[RSC])
            vop(lambda q: q.tensor_scalar(out=cB[:], in0=tc16(2), scalar1=-1.0,
                                          scalar2=1536.0, op0=ALU.mult, op1=ALU.add),
                reads=[RSC], writes=[RSC])
            vop(lambda q: q.tensor_scalar(out=u1f[:], in0=tc16(1),
                                          scalar1=mwc[:, 0:1], scalar2=1.0 / 127.0,
                                          op0=ALU.mult, op1=ALU.mult),
                reads=[RSC], writes=[RSC])
            vop(lambda q: q.tensor_scalar(out=cUh[:], in0=u1f[:], scalar1=0.5,
                                          scalar2=None, op0=ALU.mult),
                reads=[RSC], writes=[RSC])
            vop(lambda q: q.reciprocal(out=tc16(2), in_=u1f[:]),
                reads=[RSC], writes=[RSC])
            cols_to_rows(tc16(2), 16, uv1d, uv1r[0:1, :], RSC)

        def attn_scalar_prep():
            vop(lambda q: q.tensor_scalar(out=tc16(0), in0=st_sum[:],
                                          scalar1=1.0 / D, scalar2=None, op0=ALU.mult),
                reads=[RST], writes=[RSC])
            vop(lambda q: q.tensor_scalar(out=tc16(1), in0=st_sq[:],
                                          scalar1=1.0 / D, scalar2=None, op0=ALU.mult),
                reads=[RST], writes=[RSC])
            vop(lambda q: q.tensor_tensor(out=tc16(2), in0=tc16(0), in1=tc16(0),
                                          op=ALU.mult),
                reads=[RSC], writes=[RSC])
            vop(lambda q: q.tensor_tensor(out=tc16(3), in0=tc16(1), in1=tc16(2),
                                          op=ALU.subtract),
                reads=[RSC], writes=[RSC])
            vop(lambda q: q.tensor_scalar(out=tc16(2), in0=tc16(3), scalar1=EPS,
                                          scalar2=None, op0=ALU.add),
                reads=[RSC], writes=[RSC])
            vop(lambda q: q.reciprocal(out=tc16(4), in_=tc16(2)),
                reads=[RSC], writes=[RSC])
            aop(lambda q: q.activation(out=cMu[:], in_=tc16(4), func=AF.Sqrt),
                reads=[RSC], writes=[RSC])
            vop(lambda q: q.tensor_tensor(out=tc16(2), in0=tc16(0), in1=cMu[:],
                                          op=ALU.mult),
                reads=[RSC], writes=[RSC])
            vop(lambda q: q.tensor_scalar(out=cMb[:], in0=tc16(2), scalar1=-1.0,
                                          scalar2=None, op0=ALU.mult),
                reads=[RSC], writes=[RSC])

        def emit_ffn(call):
            A_.push()
            R0 = A_.alloc(128, GTC * DI, BF16)
            R0b = R0.bitcast(I8)
            gbc = A_.alloc(128, DI, BF16)
            bbc = A_.alloc(128, DI, BF16)
            WT = A_.alloc(128, 2 * 2048, BF16)
            rofm = A_.alloc(128, 512, F16)
            xq8r = A_.alloc(128, 512, I8)
            XQU = A_.alloc(128, GTC * D, BF16)
            rtan = A_.alloc(128, 512, BF16)
            rtp = A_.alloc(128, 512, BF16)
            rh1 = A_.alloc(128, 512, BF16)
            rt1 = rh1
            rz = rtan
            rsq = rtan
            rof = A_.alloc(128, 512, F32)
            RH = [Res(f"h{i}") for i in range(GTC)]
            RGB = Res("gbc")
            RWT = [Res("wt0"), Res("wt1")]
            RRm = Res("rofm")
            RX8 = Res("xq8r")
            RXQ = [Res(f"xqu{i}") for i in range(GTC)]
            RRt = Res("rtan")
            RRp = Res("rtp")
            RR1 = Res("rh1")
            RRt1 = RR1
            RRz = RRt
            RRsq = RRt
            RRo = [Res("rof")]
            RRo = [RRo[0], RRo[0]]
            RB1S = [Res("b1s0"), Res("b1s1")]
            RB2S = [Res("b2s0"), Res("b2s1")]

            kb.dma("sync", gbc[:],
                   g_bf.rearrange("(a b) -> a b", a=1).broadcast_to([128, DI]),
                   reads=[RROW], writes=[RGB], lane=2)
            kb.dma("sync", bbc[:],
                   b_bf.rearrange("(a b) -> a b", a=1).broadcast_to([128, DI]),
                   reads=[RROW], writes=[RGB], lane=2)

            ffn_scalar_prep()

            for g in range(NG):
                for t4 in range(GTC):
                    t = g * GTC + t4
                    for sl in range(4):
                        xsl = X[:, t * D + sl * 512:t * D + (sl + 1) * 512]
                        aop(lambda q, xsl=xsl, t=t: q.activation(
                            out=rofm[:], in_=xsl, func=AF.Identity,
                            scale=cA[:, t:t + 1], bias=cB[:, t:t + 1]),
                            reads=[RX[t], RSC], writes=[RRm])
                        vop(lambda q: q.tensor_scalar(
                            out=rtan[:], in0=rofm[:], scalar1=-1536.0,
                            scalar2=None, op0=ALU.add),
                            reads=[RRm], writes=[RRt])
                        bank = sl % 2
                        fns = []
                        for j in range(4):
                            fns.append(lambda q, j=j, bank=bank: q.transpose(
                                out=pbb[bank][:, j * 128:(j + 1) * 128],
                                in_=rtan[:, j * 128:(j + 1) * 128],
                                identity=eyeb[:]))
                        kb.tgroup(fns, reads=[RRt, RROW], writes=[RPB[bank]])
                        vop(lambda q, bank=bank: q.tensor_copy(
                            out=xq8r[:], in_=pbb[bank][:, 0:512]),
                            reads=[RPB[bank]], writes=[RX8])
                        gop(lambda q, t4=t4, sl=sl: q.tensor_copy(
                            out=XQU[:, t4 * D + sl * 512:t4 * D + (sl + 1) * 512],
                            in_=xq8r[:]),
                            reads=[RX8], writes=[RXQ[t4]])

                if SUB < 2:
                    continue
                for os in range(16):
                    kb.dma("sync", b1stg[0:1, (os % 2) * 512:(os % 2 + 1) * 512],
                           b1_bf[os * 512:(os + 1) * 512]
                           .rearrange("(a b) -> a b", a=1),
                           reads=[RB1S[os % 2]], writes=[RB1S[os % 2]], lane=2)
                    for kq in range(4):
                        buf = kq % 2
                        kb.dma("sync",
                               WT[:, buf * 2048:(buf + 1) * 2048]
                               .rearrange("p (k o) -> p k o", k=4),
                               w1qT[kq * 4:(kq + 1) * 4, :, os * 512:(os + 1) * 512]
                               .rearrange("k d o -> d k o"),
                               reads=[RWD], writes=[RWT[buf]], lane=buf)
                        for t4 in range(GTC):
                            fns = []
                            for j in range(4):
                                k = kq * 4 + j
                                fns.append(lambda q, k=k, j=j, buf=buf, t4=t4:
                                           q.matmul(out=pb[t4][:],
                                                    lhsT=XQU[:, t4 * D + k * 128:
                                                             t4 * D + (k + 1) * 128],
                                                    rhs=WT[:, buf * 2048 + j * 512:
                                                           buf * 2048 + (j + 1) * 512],
                                                    start=(k == 0), stop=False))
                            kb.tgroup(fns, reads=[RXQ[t4], RWT[buf]],
                                      writes=[RPB[t4]])
                    for t4 in range(GTC):
                        t = g * GTC + t4
                        kb.tgroup([lambda q, t=t, os=os, t4=t4: q.matmul(
                            out=pb[t4][:], lhsT=uv1r[0:1, t * 128:(t + 1) * 128],
                            rhs=b1stg[0:1, (os % 2) * 512:(os % 2 + 1) * 512],
                            start=False, stop=True)],
                            reads=[RROW, RB1S[os % 2]], writes=[RPB[t4]])
                        bank = t4
                        aop(lambda q, bank=bank, t=t: q.activation(
                            out=rtan[:], in_=pb[bank][:], func=AF.Tanh,
                            scale=cUh[:, t:t + 1]),
                            reads=[RPB[bank], RSC], writes=[RRt])
                        gop(lambda q: q.tensor_scalar(
                            out=rtp[:], in0=rtan[:], scalar1=1.0, scalar2=None,
                            op0=ALU.add),
                            reads=[RRt], writes=[RRp])
                        vop(lambda q, bank=bank, t=t: q.scalar_tensor_tensor(
                            out=rh1[:], in0=pb[bank][:], scalar=cUh[:, t:t + 1],
                            in1=rtp[:], op0=ALU.mult, op1=ALU.mult),
                            reads=[RPB[bank], RRp, RSC], writes=[RR1])
                        hsl = R0[:, t4 * DI + os * 512:t4 * DI + (os + 1) * 512]
                        aop(lambda q, hsl=hsl, t4=t4, os=os: q.activation(
                            out=hsl, in_=rh1[:], func=AF.Gelu,
                            accum_out=gsum[:, t4 * 16 + os:t4 * 16 + os + 1]),
                            reads=[RR1], writes=[RH[t4], RTMP])
                        aop(lambda q, hsl=hsl, t4=t4, os=os: q.activation(
                            out=rtan[:], in_=hsl, func=AF.Square,
                            accum_out=gsq[:, t4 * 16 + os:t4 * 16 + os + 1]),
                            reads=[RH[t4]], writes=[RRt, RTMP])

                def c4(i):
                    return tmpc[:, 64 + i * 4:64 + (i + 1) * 4]
                for t4 in range(GTC):
                    vop(lambda q, t4=t4: q.tensor_reduce(
                        out=c4(0)[:, t4:t4 + 1], in_=gsum[:, t4 * 16:(t4 + 1) * 16],
                        axis=AX.X, op=ALU.add),
                        reads=[RTMP], writes=[RSC])
                    vop(lambda q, t4=t4: q.tensor_reduce(
                        out=c4(1)[:, t4:t4 + 1], in_=gsq[:, t4 * 16:(t4 + 1) * 16],
                        axis=AX.X, op=ALU.add),
                        reads=[RTMP], writes=[RSC])
                vop(lambda q: q.tensor_scalar(out=c4(2), in0=c4(0),
                                              scalar1=1.0 / DI, scalar2=None,
                                              op0=ALU.mult),
                    reads=[RSC], writes=[RSC])
                vop(lambda q: q.tensor_scalar(out=c4(3), in0=c4(1),
                                              scalar1=1.0 / DI, scalar2=None,
                                              op0=ALU.mult),
                    reads=[RSC], writes=[RSC])
                vop(lambda q: q.tensor_tensor(out=c4(0), in0=c4(2), in1=c4(2),
                                              op=ALU.mult),
                    reads=[RSC], writes=[RSC])
                vop(lambda q: q.tensor_tensor(out=c4(1), in0=c4(3), in1=c4(0),
                                              op=ALU.subtract),
                    reads=[RSC], writes=[RSC])
                vop(lambda q: q.tensor_scalar(out=c4(0), in0=c4(1), scalar1=EPS,
                                              scalar2=None, op0=ALU.add),
                    reads=[RSC], writes=[RSC])
                vop(lambda q: q.reciprocal(out=c4(1), in_=c4(0)),
                    reads=[RSC], writes=[RSC])
                aop(lambda q: q.activation(out=c2A[:], in_=c4(1), func=AF.Sqrt),
                    reads=[RSC], writes=[RSC])
                vop(lambda q: q.tensor_tensor(out=c4(0), in0=c4(2), in1=c2A[:],
                                              op=ALU.mult),
                    reads=[RSC], writes=[RSC])
                vop(lambda q: q.tensor_scalar(out=c2B[:], in0=c4(0), scalar1=-1.0,
                                              scalar2=None, op0=ALU.mult),
                    reads=[RSC], writes=[RSC])

                for t4 in range(GTC):
                    t = g * GTC + t4
                    for sl in range(16):
                        hsl = R0[:, t4 * DI + sl * 512:t4 * DI + (sl + 1) * 512]
                        aop(lambda q, hsl=hsl, t4=t4: q.activation(
                            out=rt1[:], in_=hsl, func=AF.Identity,
                            scale=c2A[:, t4:t4 + 1], bias=c2B[:, t4:t4 + 1]),
                            reads=[RH[t4], RSC], writes=[RRt1])
                        gop(lambda q, sl=sl: q.tensor_tensor(
                            out=rz[:], in0=rt1[:],
                            in1=gbc[:, sl * 512:(sl + 1) * 512], op=ALU.mult),
                            reads=[RRt1, RGB], writes=[RRz])
                        gop(lambda q, hsl=hsl, sl=sl, t4=t4: q.tensor_tensor(
                            out=hsl, in0=rz[:],
                            in1=bbc[:, sl * 512:(sl + 1) * 512], op=ALU.add),
                            reads=[RRz, RGB], writes=[RH[t4]])
                        vop(lambda q, hsl=hsl, sl=sl: q.tensor_reduce(
                            out=am2[:, sl:sl + 1], in_=hsl, axis=AX.X, op=ALU.max,
                            apply_absolute_value=True),
                            reads=[RH[t4]], writes=[RTMP])
                        aop(lambda q, hsl=hsl, sl=sl: q.activation(
                            out=rsq[:], in_=hsl, func=AF.Square,
                            accum_out=ms2[:, sl:sl + 1]),
                            reads=[RH[t4]], writes=[RRsq, RTMP])
                    vop(lambda q: q.tensor_reduce(out=tmpc[:, 0:1], in_=am2[:],
                                                  axis=AX.X, op=ALU.max),
                        reads=[RTMP], writes=[RSC])
                    vop(lambda q: q.tensor_reduce(out=tmpc[:, 1:2], in_=ms2[:],
                                                  axis=AX.X, op=ALU.add),
                        reads=[RTMP], writes=[RSC])
                    vop(lambda q: q.tensor_scalar(out=tmpc[:, 2:3], in0=tmpc[:, 0:1],
                                                  scalar1=1e-5, scalar2=None,
                                                  op0=ALU.max),
                        reads=[RSC], writes=[RSC])
                    vop(lambda q: q.reciprocal(out=tmpc[:, 3:4], in_=tmpc[:, 2:3]),
                        reads=[RSC], writes=[RSC])
                    vop(lambda q, t4=t4: q.tensor_scalar(
                        out=cS2[:, t4:t4 + 1], in0=tmpc[:, 3:4], scalar1=127.0,
                        scalar2=None, op0=ALU.mult),
                        reads=[RSC], writes=[RSC])
                    vop(lambda q: q.tensor_scalar(out=tmpc[:, 4:5], in0=tmpc[:, 1:2],
                                                  scalar1=1.0 / DI, scalar2=EPS,
                                                  op0=ALU.mult, op1=ALU.add),
                        reads=[RSC], writes=[RSC])
                    vop(lambda q: q.reciprocal(out=tmpc[:, 5:6], in_=tmpc[:, 4:5]),
                        reads=[RSC], writes=[RSC])
                    aop(lambda q: q.activation(out=tmpc[:, 6:7], in_=tmpc[:, 5:6],
                                               func=AF.Sqrt),
                        reads=[RSC], writes=[RSC])
                    vop(lambda q: q.tensor_tensor(out=tmpc[:, 7:8], in0=tmpc[:, 6:7],
                                                  in1=tmpc[:, 0:1], op=ALU.mult),
                        reads=[RSC], writes=[RSC])
                    vop(lambda q: q.tensor_scalar(out=tmpc[:, 8:9], in0=tmpc[:, 7:8],
                                                  scalar1=1e-5, scalar2=None,
                                                  op0=ALU.max),
                        reads=[RSC], writes=[RSC])
                    vop(lambda q, t4=t4: q.tensor_scalar(
                        out=cU2[:, t4:t4 + 1], in0=tmpc[:, 8:9],
                        scalar1=mwc[:, 1:2], scalar2=0.5 / 127.0,
                        op0=ALU.mult, op1=ALU.mult),
                        reads=[RSC], writes=[RSC])
                vop(lambda q: q.reciprocal(out=uvt[:, 0:4], in_=cU2[:]),
                    reads=[RSC], writes=[RTMP])
                cols_to_rows(uvt[:, 0:4], 4, uv2d, uv2r[0:1, :], RTMP)

                if SUB < 5:
                    continue
                for t4 in range(GTC):
                    for sl in range(16):
                        hsl = R0[:, t4 * DI + sl * 512:t4 * DI + (sl + 1) * 512]
                        aop(lambda q, hsl=hsl, t4=t4: q.activation(
                            out=rofm[:], in_=hsl, func=AF.Identity,
                            scale=cS2[:, t4:t4 + 1], bias=halfc[:]),
                            reads=[RH[t4], RSC], writes=[RRm])
                        vop(lambda q: q.tensor_scalar(
                            out=rtan[:], in0=rofm[:], scalar1=-1536.0,
                            scalar2=None, op0=ALU.add),
                            reads=[RRm], writes=[RRt])
                        bank = sl % 2
                        if SUB >= 6:
                            fns = []
                            for j in range(4):
                                fns.append(lambda q, j=j, bank=bank: q.transpose(
                                    out=pbb[bank][:, j * 128:(j + 1) * 128],
                                    in_=rtan[:, j * 128:(j + 1) * 128],
                                    identity=eyeb[:]))
                            kb.tgroup(fns, reads=[RRt, RROW], writes=[RPB[bank]])
                        else:
                            continue
                        dst8 = R0b[:, (t4 * DI + sl * 512) * 2:
                                   (t4 * DI + sl * 512) * 2 + 512]
                        if SUB >= 8 and (g < QG or QG == 9):
                            aop(lambda q, bank=bank, dst8=dst8: q.copy(
                                out=dst8, in_=pbb[bank][:, 0:512]),
                                reads=[RPB[bank]], writes=[RH[t4]])
                        elif SUB >= 7 and (g >= 4 - QG if QG <= 4 else True):
                            aop(lambda q, bank=bank: q.copy(
                                out=xq8r[:], in_=pbb[bank][:, 0:512]),
                                reads=[RPB[bank], RX8], writes=[RX8])

                if GBAR:
                    kb.barrier()
                if SUB < 9:
                    continue
                for t4 in range(GTC):
                    t = g * GTC + t4
                    src8 = R0b[:, t4 * DI * 2:(t4 + 1) * DI * 2] \
                        .rearrange("p (s b) -> p s b", b=1024)[:, :, 0:512]
                    dst3 = XQU[:, 0:DI].rearrange("p (s b) -> p s b", b=512)
                    gop(lambda q, src8=src8, dst3=dst3: q.tensor_copy(
                        out=dst3, in_=src8),
                        reads=[RH[t4]], writes=RXQ)
                    for o2s in range(4):
                        kb.dma("sync",
                               b2stg[0:1, (o2s % 2) * 512:(o2s % 2 + 1) * 512],
                               b2h_bf[o2s * 512:(o2s + 1) * 512]
                               .rearrange("(a b) -> a b", a=1),
                               reads=[RB2S[o2s % 2]], writes=[RB2S[o2s % 2]],
                               lane=2)
                        for kq in range(16):
                            buf = kq % 2
                            kb.dma("sync",
                                   WT[:, buf * 2048:(buf + 1) * 2048]
                                   .rearrange("p (k o) -> p k o", k=4),
                                   w2qT[kq * 4:(kq + 1) * 4, :,
                                        o2s * 512:(o2s + 1) * 512]
                                   .rearrange("k d o -> d k o"),
                                   reads=[RWD], writes=[RWT[buf]], lane=buf)
                            fns = []
                            for j in range(4):
                                k = kq * 4 + j
                                fns.append(lambda q, k=k, j=j, buf=buf, t4=t4:
                                           q.matmul(out=pb[4 + t4 % 2][:],
                                                    lhsT=XQU[:, k * 128:(k + 1) * 128],
                                                    rhs=WT[:, buf * 2048 + j * 512:
                                                           buf * 2048 + (j + 1) * 512],
                                                    start=(k == 0), stop=False))
                            kb.tgroup(fns, reads=[RXQ[0], RWT[buf]],
                                      writes=[RPB[4 + t4 % 2]])
                        kb.tgroup([lambda q, t4=t4, o2s=o2s: q.matmul(
                            out=pb[4 + t4 % 2][:],
                            lhsT=uv2r[0:1, t4 * 128:(t4 + 1) * 128],
                            rhs=b2stg[0:1, (o2s % 2) * 512:(o2s % 2 + 1) * 512],
                            start=False, stop=True)],
                            reads=[RROW, RB2S[o2s % 2]], writes=[RPB[4 + t4 % 2]])
                        xsl = X[:, t * D + o2s * 512:t * D + (o2s + 1) * 512]
                        ro = rof[:, 0:512]
                        vop(lambda q, t4=t4, ro=ro, xsl=xsl, t=t: q.scalar_tensor_tensor(
                            out=ro, in0=pb[4 + t4 % 2][:], scalar=cU2[:, t4:t4 + 1],
                            in1=xsl, op0=ALU.mult, op1=ALU.add),
                            reads=[RPB[4 + t4 % 2], RX[t], RSC], writes=[RRo[0]])
                        if call == 0:
                            gop(lambda q, ro=ro, xsl=xsl: q.tensor_copy(
                                out=xsl, in_=ro),
                                reads=[RRo[0]], writes=[RX[t]])
                        else:
                            kb.dma("gpsimd",
                                   out_t[t * 128:(t + 1) * 128,
                                         o2s * 512:(o2s + 1) * 512],
                                   ro, reads=[RRo[0]], writes=[RRo[0]],
                                   lane=t4 % 2)
                if GBAR:
                    kb.barrier()
                if call == 0:
                    for t4 in range(GTC):
                        t = g * GTC + t4
                        xc = X[:, t * D:(t + 1) * D]
                        vop(lambda q, xc=xc, t=t: q.tensor_reduce(
                            out=st_sum[:, t:t + 1], in_=xc, axis=AX.X, op=ALU.add),
                            reads=[RX[t]], writes=[RST])
                        for sl in range(4):
                            aop(lambda q, xc=xc, sl=sl: q.activation(
                                out=rsq[:], in_=xc[:, sl * 512:(sl + 1) * 512],
                                func=AF.Square,
                                accum_out=tmpc[:, 24 + sl:25 + sl]),
                                reads=[RX[t]], writes=[RRsq, RTMP])
                        vop(lambda q, t=t: q.tensor_reduce(
                            out=st_sq[:, t:t + 1], in_=tmpc[:, 24:28],
                            axis=AX.X, op=ALU.add),
                            reads=[RTMP], writes=[RST])
            A_.pop()

        def emit_attn():
            A_.push()
            S0 = A_.alloc(128, H * HD, F32)
            S1 = A_.alloc(128, H * HD, F32)
            Dt = A_.alloc(128, H * 128, BF16)
            KDt = A_.alloc(128, H * HD, BF16)
            yT = A_.alloc(128, KB1 * 512, BF16)
            QT = A_.alloc(128, H * 512, BF16)
            onT = A_.alloc(128, KB1 * 512, BF16)
            WT = A_.alloc(128, 2 * 2048, BF16)
            KG = A_.alloc(128, GA * D, BF16)
            VG = A_.alloc(128, GA * D, BF16)
            KH = A_.alloc(128, D, BF16)
            Sbf = A_.alloc(128, 512, BF16)
            KTr = A_.alloc(128, 512, BF16)
            sD = KTr
            oB = A_.alloc(128, 512, BF16)
            oC = A_.alloc(128, D, BF16)
            og = A_.alloc(128, D, BF16)
            sg = A_.alloc(128, D, BF16)
            WG2 = A_.alloc(128, D, BF16)
            g1T = A_.alloc(128, GTC * 128, BF16)
            roa = A_.alloc(128, 512, F32)
            g1r = oB[:, 0:128]
            ybr = og
            dscr = roa[:, 0:128]
            RYT = Res("yT")
            RQT = Res("QT")
            RONT = Res("onT")
            RWT = [Res("awt0"), Res("awt1")]
            RKG = [Res(f"kg{i}") for i in range(GA)]
            RVG = [Res(f"vg{i}") for i in range(GA)]
            RKH = Res("kh")
            RSBF = Res("sbf")
            RKT = Res("ktr")
            RSD = RKT
            ROB = Res("ob")
            ROC = Res("oc")
            RSG = Res("sg")
            RWG2 = Res("wg2")
            RG1T = Res("g1t")
            RRo = [Res("roa")]
            RRo = [RRo[0], RRo[0]]
            RG1 = Res("obg1")
            RYB = Res("og_ybr")
            RDS = RRo[0]

            attn_scalar_prep()

            for h in range(H):
                vop(lambda q, h=h: q.tensor_scalar(
                    out=dscr[:], in0=difft[:], scalar1=ldbc[:, h:h + 1],
                    scalar2=None, op0=ALU.mult),
                    reads=[RROW, RDEC], writes=[RDS])
                aop(lambda q, h=h: q.activation(
                    out=Dt[:, h * 128:(h + 1) * 128], in_=dscr[:], func=AF.Exp),
                    reads=[RDS], writes=[RDEC])
            vop(lambda q: q.tensor_copy(
                out=KDt[:].rearrange("p (h e) -> p h e", h=H),
                in_=kdt[:].unsqueeze(2).broadcast_to([128, H, HD])),
                reads=[RDEC], writes=[RDEC])
            vop(lambda q: q.memset(S0[:], 0.0), writes=[RS_[0]])

            def build_y(t):
                tn = t % GTC if True else 0
                aop(lambda q, t=t: q.activation(
                    out=ybr[:], in_=X[:, t * D:(t + 1) * D], func=AF.Identity,
                    scale=cMu[:, t:t + 1], bias=cMb[:, t:t + 1]),
                    reads=[RX[t], RSC], writes=[RYB])
                for gblk in range(4):
                    bank = gblk % 2
                    fns = []
                    for j in range(4):
                        k = gblk * 4 + j
                        fns.append(lambda q, k=k, j=j, bank=bank:
                                   q.transpose(out=pbb[bank][:, j * 128:(j + 1) * 128],
                                               in_=ybr[:, k * 128:(k + 1) * 128],
                                               identity=eyeb[:]))
                    kb.tgroup(fns, reads=[RYB, RROW], writes=[RPB[bank]])
                    for j in range(4):
                        k = gblk * 4 + j
                        vop(lambda q, k=k, j=j, bank=bank, tn=tn: q.tensor_copy(
                            out=yT[:, k * 512 + tn * 128:k * 512 + (tn + 1) * 128],
                            in_=pbb[bank][:, j * 128:(j + 1) * 128]),
                            reads=[RPB[bank]], writes=[RYT])

            def stream_w_tiles(wsrc, os, width, dmaq="sync"):
                for kq in range(4):
                    buf = kq % 2
                    kb.dma(dmaq,
                           WT[:, buf * 2048:buf * 2048 + 4 * width]
                           .rearrange("p (k o) -> p k o", k=4),
                           wsrc[kq * 4:(kq + 1) * 4, :, os * width:(os + 1) * width]
                           .rearrange("k d o -> d k o"),
                           reads=[RWD], writes=[RWT[buf]], lane=buf)

            for g in range(NTC // GA):
                for t2 in range(GA):
                    build_y(g * GA + t2)
                for (wsrc, dscr_t, RDST, DST) in ((wkb, kscr, RKG, KG),
                                                  (wvb, vscr, RVG, VG)):
                    for os in range(4):
                        for kq in range(4):
                            buf = kq % 2
                            kb.dma("sync",
                                   WT[:, buf * 2048:(buf + 1) * 2048]
                                   .rearrange("p (k o) -> p k o", k=4),
                                   wsrc[kq * 4:(kq + 1) * 4, :,
                                        os * 512:(os + 1) * 512]
                                   .rearrange("k d o -> d k o"),
                                   reads=[RWD], writes=[RWT[buf]], lane=buf)
                            for t2 in range(GA):
                                t = g * GA + t2
                                tn = t % GTC
                                bank = 4 + t2 % 2
                                fns = []
                                for j in range(4):
                                    k = kq * 4 + j
                                    fns.append(lambda q, k=k, j=j, buf=buf,
                                               bank=bank, tn=tn:
                                               q.matmul(out=pb[bank][:],
                                                        lhsT=yT[:, k * 512 + tn * 128:
                                                                k * 512 + (tn + 1) * 128],
                                                        rhs=WT[:, buf * 2048 + j * 512:
                                                               buf * 2048 + (j + 1) * 512],
                                                        start=(k == 0), stop=(k == 15)))
                                kb.tgroup(fns, reads=[RYT, RWT[buf]],
                                          writes=[RPB[bank]])
                        for t2 in range(GA):
                            t = g * GA + t2
                            bank = 4 + t2 % 2
                            dsl = DST[:, t2 * D + os * 512:t2 * D + (os + 1) * 512]
                            aop(lambda q, bank=bank, dsl=dsl: q.activation(
                                out=dsl, in_=pb[bank][:], func=AF.Silu),
                                reads=[RPB[bank]], writes=[RDST[t2]])
                            kb.dma("gpsimd",
                                   dscr_t[t, :, os * 512:(os + 1) * 512], dsl,
                                   reads=[RDST[t2]], writes=[RWD], lane=t2 % 2)
                for t2 in range(GA):
                    t = g * GA + t2
                    cur, nxt = RS_[t % 2], RS_[(t + 1) % 2]
                    Scur = S0 if t % 2 == 0 else S1
                    Snxt = S1 if t % 2 == 0 else S0
                    gop(lambda q, t2=t2: q.tensor_tensor(
                        out=KH[:], in0=KG[:, t2 * D:(t2 + 1) * D], in1=KDt[:],
                        op=ALU.mult),
                        reads=[RKG[t2], RDEC], writes=[RKH])
                    for hg in range(4):
                        bank = 6 + hg % 2
                        fns = []
                        for hh in range(4):
                            h = hg * 4 + hh
                            fns.append(lambda q, h=h, hh=hh, bank=bank, t2=t2:
                                       q.matmul(out=pb[bank][:, hh * 128:(hh + 1) * 128],
                                                lhsT=KH[:, h * 128:(h + 1) * 128],
                                                rhs=VG[:, t2 * D + h * 128:
                                                        t2 * D + (h + 1) * 128],
                                                start=True, stop=True))
                        kb.tgroup(fns, reads=[RKH, RVG[t2]], writes=[RPB[bank]])
                        for hh in range(4):
                            h = hg * 4 + hh
                            vop(lambda q, h=h, hh=hh, bank=bank, Scur=Scur,
                                Snxt=Snxt: q.scalar_tensor_tensor(
                                    out=Snxt[:, h * 128:(h + 1) * 128],
                                    in0=Scur[:, h * 128:(h + 1) * 128],
                                    scalar=cdt[:, h:h + 1],
                                    in1=pb[bank][:, hh * 128:(hh + 1) * 128],
                                    op0=ALU.mult, op1=ALU.add),
                                reads=[cur, RPB[bank], RDEC], writes=[nxt])

            kb.dma("gpsimd", cc_in_t.ap(), S0[:], reads=[RS_[0]], writes=[RCC], lane=0)
            kb.collective("AllGather", ALU.bypass,
                          [[0, 1], [2, 3], [4, 5], [6, 7]],
                          [cc_in_t.ap().opt()], [cc_out_t.ap().opt()],
                          reads=[RCC], writes=[RCC])
            kb.dma("gpsimd", S1[:], cc_out_t.ap()[0], reads=[RCC, RS_[1]],
                   writes=[RS_[1]], lane=0)
            vop(lambda q: q.tensor_scalar(out=S0[:], in0=S1[:],
                                          scalar1=maskb[:],
                                          scalar2=None, op0=ALU.mult),
                reads=[RS_[1], RROW], writes=[RS_[0]])

            for sl in range(4):
                kb.dma("sync", roa[:, 0:512], wg2_in[:, sl * 512:(sl + 1) * 512],
                       reads=[RRo[0]], writes=[RRo[0]], lane=3)
                vop(lambda q, sl=sl: q.tensor_copy(
                    out=WG2[:, sl * 512:(sl + 1) * 512], in_=roa[:, 0:512]),
                    reads=[RRo[0]], writes=[RWG2])

            for g in range(NG):
                for t4 in range(GTC):
                    build_y(g * GTC + t4)
                for eb in range(16):
                    bank = 4 + eb % 2
                    for kq in range(4):
                        buf = kq % 2
                        kb.dma("sync",
                               WT[:, buf * 2048:buf * 2048 + 512]
                               .rearrange("p (k o) -> p k o", k=4),
                               wqb[kq * 4:(kq + 1) * 4, :, eb * 128:(eb + 1) * 128]
                               .rearrange("k d o -> d k o"),
                               reads=[RWD], writes=[RWT[buf]], lane=buf)
                        fns = []
                        for j in range(4):
                            k = kq * 4 + j
                            fns.append(lambda q, k=k, j=j, buf=buf, bank=bank:
                                       q.matmul(out=pb[bank][:],
                                                lhsT=WT[:, buf * 2048 + j * 128:
                                                        buf * 2048 + (j + 1) * 128],
                                                rhs=yT[:, k * 512:(k + 1) * 512],
                                                start=(k == 0), stop=(k == 15)))
                        kb.tgroup(fns, reads=[RYT, RWT[buf]], writes=[RPB[bank]])
                    aop(lambda q, bank=bank, eb=eb: q.activation(
                        out=QT[:, eb * 512:(eb + 1) * 512], in_=pb[bank][:],
                        func=AF.Silu),
                        reads=[RPB[bank]], writes=[RQT])
                for kq in range(4):
                    buf = kq % 2
                    kb.dma("sync",
                           WT[:, buf * 2048:buf * 2048 + 512]
                           .rearrange("p (k o) -> p k o", k=4),
                           wg1b[kq * 4:(kq + 1) * 4, :, :]
                           .rearrange("k d o -> d k o"),
                           reads=[RWD], writes=[RWT[buf]], lane=buf)
                    for t4 in range(GTC):
                        fns = []
                        for j in range(4):
                            k = kq * 4 + j
                            fns.append(lambda q, k=k, j=j, buf=buf, t4=t4:
                                       q.matmul(out=pb[4 + t4][:, 0:128],
                                                lhsT=yT[:, k * 512 + t4 * 128:
                                                        k * 512 + (t4 + 1) * 128],
                                                rhs=WT[:, buf * 2048 + j * 128:
                                                       buf * 2048 + (j + 1) * 128],
                                                start=(k == 0), stop=(k == 15)))
                        kb.tgroup(fns, reads=[RYT, RWT[buf]], writes=[RPB[4 + t4]])
                for t4 in range(GTC):
                    aop(lambda q, t4=t4: q.copy(out=g1r[:],
                                                in_=pb[4 + t4][:, 0:128]),
                        reads=[RPB[4 + t4]], writes=[ROB])
                    kb.tgroup([lambda q: q.transpose(out=pbb[0][:, 0:128],
                                                     in_=g1r[:], identity=eyeb[:])],
                              reads=[ROB, RROW], writes=[RPB[0]])
                    vop(lambda q, t4=t4: q.tensor_copy(
                        out=g1T[:, t4 * 128:(t4 + 1) * 128], in_=pbb[0][:, 0:128]),
                        reads=[RPB[0]], writes=[RG1T])
                for t4 in range(GTC):
                    t = g * GTC + t4
                    p2 = t4 % 2
                    kb.dma("sync", KG[:, p2 * D:(p2 + 1) * D], kscr[t],
                           reads=[RWD], writes=[RKG[p2]], lane=2 + p2)
                    kb.dma("sync", VG[:, p2 * D:(p2 + 1) * D], vscr[t],
                           reads=[RWD], writes=[RVG[p2]], lane=2 + p2)
                    fns = []
                    for os in range(4):
                        fns.append(lambda q, os=os, t4=t4: q.matmul(
                            out=pb[7][:, 0:512] if False else pb[7][:],
                            lhsT=g1T[:, t4 * 128:(t4 + 1) * 128],
                            rhs=WG2[:, os * 512:(os + 1) * 512],
                            start=True, stop=True) if False else q.matmul(
                            out=pb[7][:],
                            lhsT=g1T[:, t4 * 128:(t4 + 1) * 128],
                            rhs=WG2[:, os * 512:(os + 1) * 512],
                            start=True, stop=True))
                    for os in range(4):
                        kb.tgroup([lambda q, os=os, t4=t4: q.matmul(
                            out=pb[7][:],
                            lhsT=g1T[:, t4 * 128:(t4 + 1) * 128],
                            rhs=WG2[:, os * 512:(os + 1) * 512],
                            start=True, stop=True)],
                            reads=[RG1T, RWG2], writes=[RPB[7]])
                        aop(lambda q, os=os, t4=t4: q.activation(
                            out=sg[:, os * 512:(os + 1) * 512]
                            if False else sg[:, os * 512:(os + 1) * 512],
                            in_=pb[7][:], func=AF.Sigmoid),
                            reads=[RPB[7]], writes=[RSG])
                    cur, nxt = RS_[t % 2], RS_[(t + 1) % 2]
                    Scur = S0 if t % 2 == 0 else S1
                    Snxt = S1 if t % 2 == 0 else S0
                    gop(lambda q, p2=p2: q.tensor_tensor(
                        out=KH[:], in0=KG[:, p2 * D:(p2 + 1) * D], in1=KDt[:],
                        op=ALU.mult),
                        reads=[RKG[p2], RDEC], writes=[RKH])
                    for hg in range(4):
                        fns = []
                        for hh in range(4):
                            h = hg * 4 + hh
                            fns.append(lambda q, h=h, hh=hh, p2=p2: q.transpose(
                                out=pbb[0][:, hh * 128:(hh + 1) * 128],
                                in_=KG[:, p2 * D + h * 128:p2 * D + (h + 1) * 128],
                                identity=eyeb[:]))
                        kb.tgroup(fns, reads=[RKG[p2], RROW], writes=[RPB[0]])
                        vop(lambda q: q.tensor_copy(out=KTr[:], in_=pbb[0][:, 0:512]),
                            reads=[RPB[0]], writes=[RKT])
                        gop(lambda q, hg=hg, Scur=Scur: q.tensor_copy(
                            out=Sbf[:], in_=Scur[:, hg * 512:(hg + 1) * 512]),
                            reads=[cur], writes=[RSBF])
                        fns = []
                        for hh in range(4):
                            h = hg * 4 + hh
                            fns.append(lambda q, h=h, hh=hh, t4=t4: q.matmul(
                                out=pb[1][:, hh * 128:(hh + 1) * 128],
                                lhsT=KTr[:, hh * 128:(hh + 1) * 128],
                                rhs=QT[:, h * 512 + t4 * 128:h * 512 + (t4 + 1) * 128],
                                start=True, stop=True))
                        kb.tgroup(fns, reads=[RKT, RQT], writes=[RPB[1]])
                        for hh in range(4):
                            h = hg * 4 + hh
                            vop(lambda q, h=h, hh=hh: q.tensor_tensor(
                                out=sD[:, hh * 128:(hh + 1) * 128],
                                in0=pb[1][:, hh * 128:(hh + 1) * 128],
                                in1=Dt[:, h * 128:(h + 1) * 128], op=ALU.mult),
                                reads=[RPB[1], RDEC], writes=[RSD])
                        fns = []
                        for hh in range(4):
                            h = hg * 4 + hh
                            fns.append(lambda q, h=h, hh=hh, p2=p2: q.matmul(
                                out=pb[2][:, hh * 128:(hh + 1) * 128],
                                lhsT=sD[:, hh * 128:(hh + 1) * 128],
                                rhs=VG[:, p2 * D + h * 128:p2 * D + (h + 1) * 128],
                                start=True, stop=True))
                        kb.tgroup(fns, reads=[RSD, RVG[p2]], writes=[RPB[2]])
                        fns = []
                        for hh in range(4):
                            h = hg * 4 + hh
                            fns.append(lambda q, h=h, hh=hh, t4=t4: q.matmul(
                                out=pb[3][:, hh * 128:(hh + 1) * 128],
                                lhsT=QT[:, h * 512 + t4 * 128:h * 512 + (t4 + 1) * 128],
                                rhs=Sbf[:, hh * 128:(hh + 1) * 128],
                                start=True, stop=True))
                        kb.tgroup(fns, reads=[RQT, RSBF], writes=[RPB[3]])
                        aop(lambda q: q.copy(out=oB[:], in_=pb[2][:]),
                            reads=[RPB[2]], writes=[ROB])
                        for hh in range(4):
                            h = hg * 4 + hh
                            vop(lambda q, h=h, hh=hh, hg=hg: q.scalar_tensor_tensor(
                                out=oC[:, h * 128:(h + 1) * 128],
                                in0=pb[3][:, hh * 128:(hh + 1) * 128],
                                scalar=qdt[:, h:h + 1],
                                in1=oB[:, hh * 128:(hh + 1) * 128],
                                op0=ALU.mult, op1=ALU.add),
                                reads=[RPB[3], ROB, RDEC], writes=[ROC])
                        fns = []
                        for hh in range(4):
                            h = hg * 4 + hh
                            fns.append(lambda q, h=h, hh=hh, p2=p2: q.matmul(
                                out=pb[4][:, hh * 128:(hh + 1) * 128],
                                lhsT=KH[:, h * 128:(h + 1) * 128],
                                rhs=VG[:, p2 * D + h * 128:p2 * D + (h + 1) * 128],
                                start=True, stop=True))
                        kb.tgroup(fns, reads=[RKH, RVG[p2]], writes=[RPB[4]])
                        for hh in range(4):
                            h = hg * 4 + hh
                            vop(lambda q, h=h, hh=hh, Scur=Scur, Snxt=Snxt:
                                q.scalar_tensor_tensor(
                                    out=Snxt[:, h * 128:(h + 1) * 128],
                                    in0=Scur[:, h * 128:(h + 1) * 128],
                                    scalar=cdt[:, h:h + 1],
                                    in1=pb[4][:, hh * 128:(hh + 1) * 128],
                                    op0=ALU.mult, op1=ALU.add),
                                reads=[cur, RPB[4], RDEC], writes=[nxt])
                    if SDBG:
                        kb.dma("gpsimd", sdbg_t[t], Snxt[:], reads=[nxt],
                               writes=[RCC], lane=0)
                    if SDBG:
                        kb.dma("gpsimd", odbg_t[t], oC[:], reads=[ROC],
                               writes=[RCC], lane=0)
                        kb.dma("gpsimd", gdbg_t[t], sg[:], reads=[RSG],
                               writes=[RCC], lane=0)
                    gop(lambda q: q.tensor_tensor(out=og[:], in0=oC[:], in1=sg[:],
                                                  op=ALU.mult),
                        reads=[ROC, RSG], writes=[RYB])
                    gop(lambda q: q.tensor_tensor(out=oC[:], in0=og[:], in1=og[:],
                                                  op=ALU.mult),
                        reads=[RYB], writes=[ROC])
                    vop(lambda q: q.tensor_reduce(
                        out=tmpc[:, 0:16],
                        in_=oC[:].rearrange("p (h e) -> p h e", h=H),
                        axis=AX.X, op=ALU.add),
                        reads=[ROC], writes=[RTMP])
                    vop(lambda q: q.tensor_scalar(out=tmpc[:, 16:32],
                                                  in0=tmpc[:, 0:16],
                                                  scalar1=1.0 / HD, scalar2=EPS,
                                                  op0=ALU.mult, op1=ALU.add),
                        reads=[RTMP], writes=[RTMP])
                    vop(lambda q: q.reciprocal(out=tmpc[:, 32:48],
                                               in_=tmpc[:, 16:32]),
                        reads=[RTMP], writes=[RTMP])
                    aop(lambda q: q.activation(out=tmpc[:, 48:64],
                                               in_=tmpc[:, 32:48], func=AF.Sqrt),
                        reads=[RTMP], writes=[RTMP])
                    vop(lambda q: q.tensor_tensor(
                        out=oC[:].rearrange("p (h e) -> p h e", h=H),
                        in0=og[:].rearrange("p (h e) -> p h e", h=H),
                        in1=tmpc[:, 48:64].unsqueeze(2).broadcast_to([128, H, HD]),
                        op=ALU.mult),
                        reads=[RYB, RTMP], writes=[ROC])
                    if SDBG:
                        kb.dma("gpsimd", ndbg_t[t], oC[:], reads=[ROC],
                               writes=[RCC], lane=0)
                    for gblk in range(4):
                        bank = 5
                        fns = []
                        for j in range(4):
                            k = gblk * 4 + j
                            fns.append(lambda q, k=k, j=j: q.transpose(
                                out=pbb[5][:, j * 128:(j + 1) * 128],
                                in_=oC[:, k * 128:(k + 1) * 128],
                                identity=eyeb[:]))
                        kb.tgroup(fns, reads=[ROC, RROW], writes=[RPB[5]])
                        for j in range(4):
                            k = gblk * 4 + j
                            vop(lambda q, k=k, j=j, t4=t4: q.tensor_copy(
                                out=onT[:, k * 512 + t4 * 128:k * 512 + (t4 + 1) * 128],
                                in_=pbb[5][:, j * 128:(j + 1) * 128]),
                                reads=[RPB[5]], writes=[RONT])
                for os in range(4):
                    for kq in range(4):
                        buf = kq % 2
                        kb.dma("sync",
                               WT[:, buf * 2048:(buf + 1) * 2048]
                               .rearrange("p (k o) -> p k o", k=4),
                               wob[kq * 4:(kq + 1) * 4, :, os * 512:(os + 1) * 512]
                               .rearrange("k d o -> d k o"),
                               reads=[RWD], writes=[RWT[buf]], lane=buf)
                        for t4 in range(GTC):
                            bank = 4 + t4
                            fns = []
                            for j in range(4):
                                k = kq * 4 + j
                                fns.append(lambda q, k=k, j=j, buf=buf, bank=bank,
                                           t4=t4:
                                           q.matmul(out=pb[bank][:],
                                                    lhsT=onT[:, k * 512 + t4 * 128:
                                                            k * 512 + (t4 + 1) * 128],
                                                    rhs=WT[:, buf * 2048 + j * 512:
                                                           buf * 2048 + (j + 1) * 512],
                                                    start=(k == 0), stop=(k == 15)))
                            kb.tgroup(fns, reads=[RONT, RWT[buf]],
                                      writes=[RPB[bank]])
                    for t4 in range(GTC):
                        t = g * GTC + t4
                        bank = 4 + t4
                        xsl = X[:, t * D + os * 512:t * D + (os + 1) * 512]
                        ro = roa[:, 0:512]
                        vop(lambda q, bank=bank, ro=ro, xsl=xsl: q.tensor_tensor(
                            out=ro, in0=pb[bank][:], in1=xsl, op=ALU.add),
                            reads=[RPB[bank], RX[t]], writes=[RRo[0]])
                        gop(lambda q, ro=ro, xsl=xsl: q.tensor_copy(
                            out=xsl, in_=ro),
                            reads=[RRo[0]], writes=[RX[t]])
                for t4 in range(GTC):
                    t = g * GTC + t4
                    xc = X[:, t * D:(t + 1) * D]
                    vop(lambda q, xc=xc, t=t: q.tensor_reduce(
                        out=st_sum[:, t:t + 1], in_=xc, axis=AX.X, op=ALU.add),
                        reads=[RX[t]], writes=[RST])
                    vop(lambda q, xc=xc, t=t: q.tensor_reduce(
                        out=st_mx[:, t:t + 1], in_=xc, axis=AX.X, op=ALU.max),
                        reads=[RX[t]], writes=[RST])
                    vop(lambda q, xc=xc, t=t: q.tensor_reduce(
                        out=st_mn[:, t:t + 1], in_=xc, axis=AX.X, op=ALU.min),
                        reads=[RX[t]], writes=[RST])
                    for sl in range(4):
                        aop(lambda q, xc=xc, sl=sl: q.activation(
                            out=oB[:], in_=xc[:, sl * 512:(sl + 1) * 512],
                            func=AF.Square, accum_out=tmpc[:, 24 + sl:25 + sl]),
                            reads=[RX[t]], writes=[ROB, RTMP])
                    vop(lambda q, t=t: q.tensor_reduce(
                        out=st_sq[:, t:t + 1], in_=tmpc[:, 24:28],
                        axis=AX.X, op=ALU.add),
                        reads=[RTMP], writes=[RST])
            A_.pop()

        if STAGE >= 1:
            emit_ffn(0)
            kb.barrier()
        if STAGE >= 2:
            emit_attn()
            kb.barrier()
        if STAGE >= 3:
            emit_ffn(1)
        else:
            A_.push()
            ost = A_.alloc(128, 2 * D, F32)
            RO = [Res("o0"), Res("o1")]
            for t in range(NTC):
                sl = ost[:, (t % 2) * D:(t % 2 + 1) * D]
                vop(lambda q, sl=sl, t=t: q.tensor_copy(
                    out=sl, in_=X[:, t * D:(t + 1) * D]),
                    reads=[RX[t]], writes=[RO[t % 2]])
                kb.dma("sync", out_t[t * 128:(t + 1) * 128, :], sl,
                       reads=[RO[t % 2]], writes=[RO[t % 2]], lane=t % 2)
            A_.pop()
        kb.barrier()

        kb.finalize(block)
    return nc


_NC_CACHE = {}


def kernel(**inputs):
    x = np.ascontiguousarray(inputs["L0_attn_x"])
    key = STAGE
    if key not in _NC_CACHE:
        _NC_CACHE[key] = build_nc()
    nc = _NC_CACHE[key]

    eyef = np.eye(128, dtype=np.float32)
    eyeb = np.eye(128, dtype=ml_dtypes.bfloat16)
    onesr = np.ones((1, 128), np.float32)
    onesb = np.ones((1, 128), ml_dtypes.bfloat16)
    onesc = np.ones((128, 128), np.float32)
    idx = np.arange(128, dtype=np.float32)
    difft = idx[None, :] - idx[:, None]
    difft = np.where(difft >= 0, difft, 1e9).astype(np.float32)
    pos1 = (idx + 1).reshape(128, 1).astype(np.float32)
    posk = (127 - idx).reshape(128, 1).astype(np.float32)

    base = dict(
        log_decay=np.ascontiguousarray(inputs["L0_attn_log_decay"]),
        Wq=np.ascontiguousarray(inputs["L0_attn_Wq"]),
        Wk=np.ascontiguousarray(inputs["L0_attn_Wk"]),
        Wv=np.ascontiguousarray(inputs["L0_attn_Wv"]),
        Wo=np.ascontiguousarray(inputs["L0_attn_Wo"]),
        Wg1=np.ascontiguousarray(inputs["L0_attn_Wg1"]),
        Wg2=np.ascontiguousarray(inputs["L0_attn_Wg2"]),
        norm_w=np.ascontiguousarray(inputs["L0_attn_norm_w"]),
        w1=np.ascontiguousarray(inputs["L0_ffn_w1"]),
        b1=np.ascontiguousarray(inputs["L0_ffn_b1"]),
        ln_g=np.ascontiguousarray(inputs["L0_ffn_ln_g"]),
        ln_b=np.ascontiguousarray(inputs["L0_ffn_ln_b"]),
        w2=np.ascontiguousarray(inputs["L0_ffn_w2"]),
        b2=np.ascontiguousarray(inputs["L0_ffn_b2"]),
        eyef=eyef, eyeb=eyeb, onesr=onesr, onesb=onesb, onesc=onesc,
        difft=difft, pos1=pos1, posk=posk,
    )
    in_maps = []
    for c in range(8):
        bb = c // 2
        hh = c % 2
        m = dict(base)
        m["x"] = np.ascontiguousarray(x[bb, hh * TPC:(hh + 1) * TPC, :])
        m["mask"] = np.array([[float(hh)]], np.float32)
        in_maps.append(m)

    res = run_bass_kernel_spmd(nc, in_maps, core_ids=list(range(8)))
    out = np.empty((B, N, D), np.float32)
    for c in range(8):
        bb = c // 2
        hh = c % 2
        out[bb, hh * TPC:(hh + 1) * TPC, :] = res.results[c]["out"]
    return out


# version 04 (final)
# speedup vs baseline: 1.5302x; 1.5302x over previous
import os
import numpy as np
import ml_dtypes
import concourse.bass as bass
import concourse.mybir as mybir
from concourse.bass_utils import run_bass_kernel_spmd
from contextlib import ExitStack

F32 = mybir.dt.float32
F16 = mybir.dt.float16
BF16 = mybir.dt.bfloat16
I8 = mybir.dt.int8
AF = mybir.ActivationFunctionType
ALU = mybir.AluOpType
AX = mybir.AxisListType

B, N, D, H, HD = 4, 4096, 2048, 16, 128
TPC = 2048
NTC = 16
DI = 8192
KB1 = 16
KB2 = 64
GTC = 4
NG = NTC // GTC
GA = 2
EPS = 1e-5

STAGE = int(os.environ.get("MK_STAGE", "3"))
SUB = int(os.environ.get("MK_SUB", "9"))
SDBG = bool(os.environ.get("MK_SDBG"))
QG = int(os.environ.get("MK_QG", "4"))
GBAR = int(os.environ.get("MK_GBAR", "0"))

ENGINES = ("sync", "tensor", "vector", "scalar", "gpsimd")


class Res:
    __slots__ = ("name", "w", "rs")

    def __init__(self, name):
        self.name = name
        self.w = None
        self.rs = []


class KB:
    DMA_LANES = {"sync": 4, "gpsimd": 2, "scalar": 2}

    def __init__(self, nc, ctx):
        self.nc = nc
        self.q = {e: [] for e in ENGINES}
        self.sem = {}
        self.dsems = set()
        for name in ("pT", "pV", "pA", "pG", "cc"):
            h = ctx.enter_context(nc.semaphore(name))
            self.sem[name] = [h, 0]
        self.qpre = {"sync": "dS", "gpsimd": "dG", "scalar": "dC"}
        for qn, nl in self.DMA_LANES.items():
            for l in range(nl):
                name = f"{self.qpre[qn]}{l}"
                h = ctx.enter_context(nc.semaphore(name))
                self.sem[name] = [h, 0]
                self.dsems.add(name)
        self.esem = {"tensor": "pT", "vector": "pV", "scalar": "pA",
                     "gpsimd": "pG"}
        self.waited = {e: {} for e in ENGINES}
        self.nops = 0

    def _wait(self, eng, sname, val):
        if self.waited[eng].get(sname, -1) >= val:
            return
        if os.environ.get("MK_DBG2") and (eng, sname, val) == tuple(
                os.environ["MK_DBG2"].split(",", 2)[:2]) + (int(os.environ["MK_DBG2"].split(",")[2]),):
            import traceback
            traceback.print_stack()
        assert val <= self.sem[sname][1], \
            f"{eng} waits {sname}>={val} > planned {self.sem[sname][1]}"
        h = self.sem[sname][0]
        self.q[eng].append(lambda q, h=h, v=val: q.wait_ge(h, v))
        self.waited[eng][sname] = val

    def _deps(self, eng, reads, writes):
        for r in reads:
            if r.w is not None:
                self._wait(eng, r.w[0], r.w[1])
        for w in writes:
            if w.w is not None:
                self._wait(eng, w.w[0], w.w[1])
            for (sn, v) in w.rs:
                self._wait(eng, sn, v)

    def _record(self, sname, val, reads, writes):
        for r in reads:
            r.rs.append((sname, val))
        for w in writes:
            w.w = (sname, val)
            w.rs = []

    def eng(self, eng, fn, reads=(), writes=()):
        self._deps(eng, reads, writes)
        sname = self.esem[eng]
        s = self.sem[sname]
        s[1] += 1
        v = s[1]
        h = s[0]

        if os.environ.get("MK_DBG"):
            import traceback
            tb = traceback.extract_stack()[-3]

            def w(q, fn=fn, h=h, eng=eng, v=v, tb=tb):
                before = sum(len(b.instructions) for b in self.nc.m.functions[0].blocks)
                fn(q).then_inc(h, 1)
                after = sum(len(b.instructions) for b in self.nc.m.functions[0].blocks)
                if after - before != 1:
                    print(f"SPLIT OP: {eng} v={v} n={after-before} at {tb.filename}:{tb.lineno}")
        else:
            def w(q, fn=fn, h=h):
                fn(q).then_inc(h, 1)
        self.q[eng].append(w)
        self.nops += 1
        self._record(sname, v, reads, writes)
        return v

    def tgroup(self, fns, reads=(), writes=()):
        self._deps("tensor", reads, writes)
        s = self.sem["pT"]
        s[1] += 1
        v = s[1]
        h = s[0]
        for i, fn in enumerate(fns):
            if i == len(fns) - 1:
                def w(q, fn=fn, h=h):
                    fn(q).then_inc(h, 1)
                self.q["tensor"].append(w)
            else:
                self.q["tensor"].append(fn)
        self.nops += len(fns)
        self._record("pT", v, reads, writes)
        return v

    def dma(self, qname, out, in_, reads=(), writes=(), slow=False, lane=0):
        self._deps(qname, reads, writes)
        sname = f"{self.qpre[qname]}{lane % self.DMA_LANES[qname]}"
        s = self.sem[sname]
        self._wait(qname, sname, s[1])
        s[1] += 16
        v = s[1]
        h = s[0]
        self.q[qname].append(
            lambda q, o=out, i=in_, h=h, sl=slow: q.dma_start(
                out=o, in_=i, allow_slow_non_contiguous=sl).then_inc(h, 16))
        self.nops += 1
        self._record(sname, v, reads, writes)
        return v

    def collective(self, kind, op, groups, ins, outs, reads=(), writes=()):
        self._deps("gpsimd", reads, writes)
        s = self.sem["cc"]
        s[1] += 1
        v = s[1]
        h = s[0]

        def w(q):
            q.collective_compute(kind, op, replica_groups=groups,
                                 ins=ins, outs=outs).then_inc(h)
        self.q["gpsimd"].append(w)
        self._record("cc", v, reads, writes)
        return v

    def barrier(self):
        vals = {}
        for e in ("tensor", "vector", "scalar", "gpsimd"):
            sn = self.esem[e]
            s = self.sem[sn]
            self.q[e].append(lambda q, h=s[0], v=s[1]: q.wait_ge(h, v))
            s[1] += 1
            vals[sn] = s[1]
            h = s[0]
            self.q[e].append(lambda q, h=h: q.sem_inc(h, 1))
        for sn in self.dsems:
            vals[sn] = self.sem[sn][1]
        for e in ENGINES:
            for sn, v in vals.items():
                if sn == self.esem.get(e) or v == 0:
                    continue
                if self.waited[e].get(sn, -1) >= v:
                    continue
                h = self.sem[sn][0]
                self.q[e].append(lambda q, h=h, v=v: q.wait_ge(h, v))
                self.waited[e][sn] = v

    def finalize(self, block):
        for e in ENGINES:
            fns = self.q[e]
            sec = getattr(block, e)

            def runner(q, fns=fns):
                for f in fns:
                    f(q)
            sec(runner)


class Arena:
    def __init__(self, ap, nbytes):
        self.ap = ap
        self.nbytes = nbytes
        self.off = 0
        self.marks = []
        self.hi = 0

    def alloc(self, rows, n, dt):
        per = n * mybir.dt.size(dt)
        off = (self.off + 31) & ~31
        assert off + per <= self.nbytes, \
            f"arena overflow: {off}+{per} > {self.nbytes}"
        self.off = off + per
        self.hi = max(self.hi, self.off)
        return self.ap[0:rows, off // 2:(off + per) // 2].bitcast(dt)

    def push(self):
        self.marks.append(self.off)

    def pop(self):
        self.off = self.marks.pop()


def build_nc():
    nc = bass.Bass("TRN2", target_bir_lowering=False)

    def din(n, sh, dt=F32):
        return nc.dram_tensor(n, sh, dt, kind="ExternalInput").ap()

    x_in = din("x", [TPC, D])
    ld_in = din("log_decay", [H])
    wq_in = din("Wq", [D, D])
    wk_in = din("Wk", [D, D])
    wv_in = din("Wv", [D, D])
    wo_in = din("Wo", [D, D])
    wg1_in = din("Wg1", [D, HD])
    wg2_in = din("Wg2", [HD, D])
    nw_in = din("norm_w", [D])
    w1_in = din("w1", [DI, D])
    b1_in = din("b1", [DI])
    lng_in = din("ln_g", [DI])
    lnb_in = din("ln_b", [DI])
    w2_in = din("w2", [D, DI])
    b2_in = din("b2", [D])
    mask_in = din("mask", [1, 1])
    eyef_in = din("eyef", [128, 128])
    eyeb_in = din("eyeb", [128, 128], BF16)
    onesr_in = din("onesr", [1, 128])
    onesb_in = din("onesb", [1, 128], BF16)
    onesc_in = din("onesc", [128, 128])
    diff_in = din("difft", [128, 128])
    pos1_in = din("pos1", [128, 1])
    posk_in = din("posk", [128, 1])

    out_t = nc.dram_tensor("out", [TPC, D], F32, kind="ExternalOutput").ap()

    w1qT = nc.dram_tensor("w1qT", [KB1, 128, DI], BF16).ap()
    w2qT = nc.dram_tensor("w2qT", [KB2, 128, D], BF16).ap()
    wqb = nc.dram_tensor("wqb", [KB1, 128, D], BF16).ap()
    wkb = nc.dram_tensor("wkb", [KB1, 128, D], BF16).ap()
    wvb = nc.dram_tensor("wvb", [KB1, 128, D], BF16).ap()
    wob = nc.dram_tensor("wob", [KB1, 128, D], BF16).ap()
    wg1b = nc.dram_tensor("wg1b", [KB1, 128, HD], BF16).ap()
    kscr = nc.dram_tensor("kscr", [NTC, 128, D], BF16).ap()
    vscr = nc.dram_tensor("vscr", [NTC, 128, D], BF16).ap()
    sdbg_t = nc.dram_tensor("sdbg", [NTC, 128, H * HD], F32).ap()
    odbg_t = nc.dram_tensor("odbg", [NTC, 128, D], F32).ap()
    ndbg_t = nc.dram_tensor("ndbg", [NTC, 128, D], F32).ap()
    gdbg_t = nc.dram_tensor("gdbg", [NTC, 128, D], F32).ap()
    cc_in_t = nc.dram_tensor("cc_in", [128, H * HD], F32)
    cc_out_t = nc.dram_tensor("cc_out", [2, 128, H * HD], F32)
    g_bf = nc.dram_tensor("g_bf", [DI], BF16).ap()
    b_bf = nc.dram_tensor("b_bf", [DI], BF16).ap()
    b1_bf = nc.dram_tensor("b1_bf", [DI], BF16).ap()
    b2h_bf = nc.dram_tensor("b2h_bf", [D], BF16).ap()
    uv1d = nc.dram_tensor("uv1d", [16, 128], BF16).ap()
    uv2d = nc.dram_tensor("uv2d", [16, 128], BF16).ap()

    with ExitStack() as ctx:
        kb = KB(nc, ctx)

        def sb(name, sh, dt):
            return ctx.enter_context(nc.sbuf_tensor(name, sh, dt))

        X = sb("X", [128, NTC * D], BF16)
        st_sum = sb("st_sum", [128, NTC], F32)
        st_sq = sb("st_sq", [128, NTC], F32)
        st_mx = sb("st_mx", [128, NTC], F32)
        st_mn = sb("st_mn", [128, NTC], F32)
        cA = sb("cA", [128, NTC], F32)
        cB = sb("cB", [128, NTC], F32)
        cUh = sb("cUh", [128, NTC], F32)
        u1f = sb("u1f", [128, NTC], F32)
        cMu = sb("cMu", [128, NTC], F32)
        cMb = sb("cMb", [128, NTC], F32)
        rows16 = sb("rows16", [16, 128], BF16)
        rows16f = sb("rows16f", [16, 128], F32)
        uv1r = sb("uv1r", [1, 16 * 128], BF16)
        uv2r = sb("uv2r", [1, 4 * 128], BF16)
        b1stg = sb("b1stg", [1, 2 * 512], BF16)
        b2stg = sb("b2stg", [1, 2 * 512], BF16)
        swc = sb("swc", [128, 2], F32)
        mwc = sb("mwc", [128, 2], F32)
        maskb = sb("maskb", [128, 1], F32)
        eyef = sb("s_eyef", [128, 128], F32)
        eyeb = sb("s_eyeb", [128, 128], BF16)
        onesr = sb("s_onesr", [1, 128], F32)
        onesb = sb("s_onesb", [1, 128], BF16)
        onesc = sb("s_onesc", [128, 128], F32)
        difft = sb("s_difft", [128, 128], F32)
        pos1 = sb("s_pos1", [128, 1], F32)
        posk = sb("s_posk", [128, 1], F32)
        ldbc = sb("ldbc", [128, H], F32)
        qdt = sb("qdt", [128, H], F32)
        kdt = sb("kdt", [128, H], F32)
        cdt = sb("cdt", [128, H], F32)
        tmpc = sb("tmpc", [128, 96], F32)
        gsum = sb("gsum", [128, 64], F32)
        gsq = sb("gsq", [128, 64], F32)
        am2 = sb("am2", [128, 16], F32)
        ms2 = sb("ms2", [128, 16], F32)
        c2A = sb("c2A", [128, GTC], F32)
        c2B = sb("c2B", [128, GTC], F32)
        cS2 = sb("cS2", [128, GTC], F32)
        cU2 = sb("cU2", [128, GTC], F32)
        nwcol = sb("nwcol", [128, KB1], F32)
        halfc = sb("halfc", [128, 1], F32)
        wsum = sb("wsum", [128, 64], F32)
        uvt = sb("uvt", [128, 16], F32)

        AR = sb("ARENA", [128, 66064], BF16)
        A_ = Arena(AR[:], 132128)

        pb = [ctx.enter_context(nc.psum_tensor(f"pb{i}", [128, 512], F32))
              for i in range(8)]
        pbb = [p[:].bitcast(BF16) for p in pb]

        RX = [Res(f"x{t}") for t in range(NTC)]
        RST = Res("stats")
        RSC = Res("scal")
        RPB = [Res(f"pb{i}") for i in range(8)]
        RROW = Res("rows")
        RWD = Res("wscr_dram")
        RS_ = [Res("S0"), Res("S1")]
        RDEC = Res("decays")
        RCC = Res("ccres")
        RTMP = Res("tmpcols")

        block = ctx.enter_context(nc.Block())

        def vop(fn, reads=(), writes=()):
            return kb.eng("vector", fn, reads, writes)

        def aop(fn, reads=(), writes=()):
            return kb.eng("scalar", fn, reads, writes)

        def gop(fn, reads=(), writes=()):
            return kb.eng("gpsimd", fn, reads, writes)

        def bcast_row(src_ap, dst_ap, n, bank, rsrc, rdst):
            lhs = onesr[0:1, :] if src_ap.dtype == F32 else onesb[0:1, :]
            kb.tgroup([lambda q, lhs=lhs, src=src_ap, n=n, bank=bank: q.matmul(
                out=pb[bank][:, 0:n], lhsT=lhs, rhs=src, start=True, stop=True)],
                reads=[rsrc, RROW], writes=[RPB[bank]])
            kb.eng("vector", lambda q, dst=dst_ap, n=n, bank=bank: q.tensor_copy(
                out=dst, in_=pb[bank][:, 0:n]),
                reads=[RPB[bank]], writes=[rdst])

        def cols_to_rows(cols_ap, ncol, dscratch, row_ap, rres, bank=6):
            kb.tgroup([lambda q: q.transpose(out=pb[bank][0:ncol, 0:128],
                                             in_=cols_ap, identity=eyef[:])],
                      reads=[rres, RROW], writes=[RPB[bank]])
            vop(lambda q: q.tensor_copy(out=rows16[0:ncol, :],
                                        in_=pb[bank][0:ncol, 0:128]),
                reads=[RPB[bank]], writes=[RROW])
            kb.dma("sync", dscratch[0:ncol, :], rows16[0:ncol, :],
                   reads=[RROW], writes=[RROW], lane=3)
            kb.dma("sync", row_ap,
                   dscratch.rearrange("a b -> (a b)")[0:ncol * 128]
                   .rearrange("(a b) -> a b", a=1),
                   reads=[RROW], writes=[RROW], lane=3)

        for dst, src in [(eyef[:], eyef_in), (eyeb[:], eyeb_in),
                         (onesr[:], onesr_in), (onesb[:], onesb_in),
                         (onesc[:], onesc_in), (difft[:], diff_in),
                         (pos1[:], pos1_in), (posk[:], posk_in)]:
            kb.dma("sync", dst, src, writes=[RROW], lane=3)
        kb.dma("sync", rows16f[:], nw_in.rearrange("(k p) -> k p", p=128),
               writes=[RROW], lane=3)
        kb.tgroup([lambda q: q.transpose(out=pb[6][:, 0:16],
                                         in_=rows16f[:], identity=eyef[0:16, 0:16])],
                  reads=[RROW], writes=[RPB[6]])
        kb.eng("vector", lambda q: q.tensor_copy(out=nwcol[:], in_=pb[6][:, 0:16]),
               reads=[RPB[6]], writes=[RROW])
        kb.dma("sync", ldbc[:],
               ld_in.rearrange("(a b) -> a b", a=1).broadcast_to([128, H]),
               writes=[RDEC], lane=3)
        kb.dma("sync", maskb[:], mask_in.broadcast_to([128, 1]), writes=[RROW], lane=3)

        A_.push()
        rowstg = A_.alloc(16, 512, F32)
        rowbf = A_.alloc(16, 512, BF16)
        RRS = Res("rowstg")
        RRB = Res("rowbf")
        for dst, src, nr in [(b1_bf, b1_in, 16), (g_bf, lng_in, 16),
                             (b_bf, lnb_in, 16)]:
            kb.dma("sync", rowstg[0:nr, :],
                   src.rearrange("(a b) -> a b", a=nr), reads=[RRS], writes=[RRS], lane=3)
            vop(lambda q, nr=nr: q.tensor_copy(out=rowbf[0:nr, :],
                                               in_=rowstg[0:nr, :]),
                reads=[RRS], writes=[RRB])
            kb.dma("sync", dst.rearrange("(a b) -> a b", a=nr), rowbf[0:nr, :],
                   reads=[RRB], writes=[RROW], lane=3)
        kb.dma("sync", rowstg[0:4, :], b2_in.rearrange("(a b) -> a b", a=4),
               reads=[RRS], writes=[RRS], lane=3)
        vop(lambda q: q.tensor_scalar(out=rowbf[0:4, :], in0=rowstg[0:4, :],
                                      scalar1=0.5, scalar2=None, op0=ALU.mult),
            reads=[RRS], writes=[RRB])
        kb.dma("sync", b2h_bf.rearrange("(a b) -> a b", a=4), rowbf[0:4, :],
               reads=[RRB], writes=[RROW], lane=3)

        vop(lambda q: q.tensor_tensor(
            out=tmpc[:, 0:H], in0=ldbc[:],
            in1=pos1[:, 0:1].broadcast_to([128, H]), op=ALU.mult),
            reads=[RDEC, RROW], writes=[RTMP])
        aop(lambda q: q.activation(out=qdt[:], in_=tmpc[:, 0:H], func=AF.Exp),
            reads=[RTMP], writes=[RDEC])
        vop(lambda q: q.tensor_tensor(
            out=tmpc[:, 0:H], in0=ldbc[:],
            in1=posk[:, 0:1].broadcast_to([128, H]), op=ALU.mult),
            reads=[RDEC, RROW, RTMP], writes=[RTMP])
        aop(lambda q: q.activation(out=kdt[:], in_=tmpc[:, 0:H], func=AF.Exp),
            reads=[RTMP], writes=[RDEC])
        aop(lambda q: q.activation(out=cdt[:], in_=ldbc[:], func=AF.Exp,
                                   scale=128.0),
            reads=[RDEC], writes=[RDEC])

        vop(lambda q: q.memset(halfc[:], 1536.0), writes=[RROW])

        xstg = A_.alloc(128, 2 * D, F32)
        RXS = [Res("xs0"), Res("xs1")]
        sqst = A_.alloc(128, 2 * D, BF16)
        RSQ2 = [Res("sq0"), Res("sq1")]
        for t in range(NTC):
            sl = xstg[:, (t % 2) * D:(t % 2 + 1) * D]
            kb.dma("sync", sl, x_in[t * 128:(t + 1) * 128, :],
                   reads=[RXS[t % 2]], writes=[RXS[t % 2]], lane=t % 2)
            vop(lambda q, sl=sl, t=t: q.tensor_reduce(
                out=st_sum[:, t:t + 1], in_=sl, axis=AX.X, op=ALU.add),
                reads=[RXS[t % 2]], writes=[RST])
            vop(lambda q, sl=sl, t=t: q.tensor_reduce(
                out=st_mx[:, t:t + 1], in_=sl, axis=AX.X, op=ALU.max),
                reads=[RXS[t % 2]], writes=[RST])
            vop(lambda q, sl=sl, t=t: q.tensor_reduce(
                out=st_mn[:, t:t + 1], in_=sl, axis=AX.X, op=ALU.min),
                reads=[RXS[t % 2]], writes=[RST])
            sq = sqst[:, (t % 2) * D:(t % 2 + 1) * D]
            aop(lambda q, sl=sl, sq=sq, t=t: q.activation(
                out=sq, in_=sl, func=AF.Square, accum_out=st_sq[:, t:t + 1]),
                reads=[RXS[t % 2]], writes=[RSQ2[t % 2], RST])
            aop(lambda q, sl=sl, t=t: q.copy(
                out=X[:, t * D:(t + 1) * D], in_=sl),
                reads=[RXS[t % 2]], writes=[RX[t]])

        wstg = A_.alloc(128, 2 * D, F32)
        RWS = [Res("ws0"), Res("ws1")]
        qstg = A_.alloc(128, D, F32)
        RQS = Res("qs")
        qstg2 = A_.alloc(128, D, F16)
        RQ2 = Res("qs2")
        qbs = A_.alloc(128, D, BF16)
        RQB = Res("qb")
        tstg = A_.alloc(128, 2 * D, BF16)
        RTS = [Res("ts0"), Res("ts1")]

        def w_scale(w_ap, rows, cols, col_i, dmaq):
            ntile = (rows // 128) * (cols // D)
            assert ntile == 64
            for i in range(ntile):
                r = i // (cols // D)
                c = i % (cols // D)
                sl = wstg[:, (i % 2) * D:(i % 2 + 1) * D]
                kb.dma(dmaq, sl, w_ap[r * 128:(r + 1) * 128, c * D:(c + 1) * D],
                       reads=[RWS[i % 2]], writes=[RWS[i % 2]], lane=i % 2)
                vop(lambda q, sl=sl, i=i: q.tensor_reduce(
                    out=wsum[:, i:i + 1], in_=sl, axis=AX.X, op=ALU.add,
                    apply_absolute_value=True),
                    reads=[RWS[i % 2]], writes=[RTMP])
            vop(lambda q: q.tensor_reduce(
                out=tmpc[:, 0:1], in_=wsum[:], axis=AX.X, op=ALU.add),
                reads=[RTMP], writes=[RTMP])
            kb.tgroup([lambda q: q.matmul(out=pb[7][:, 0:1], lhsT=onesc[:],
                                          rhs=tmpc[:, 0:1], start=True, stop=True)],
                      reads=[RTMP, RROW], writes=[RPB[7]])
            cnt = float(rows * cols)
            vop(lambda q, col_i=col_i, cnt=cnt: q.tensor_scalar(
                out=mwc[:, col_i:col_i + 1], in0=pb[7][:, 0:1],
                scalar1=1.0 / cnt, scalar2=1e-5, op0=ALU.mult, op1=ALU.max),
                reads=[RPB[7]], writes=[RSC])
            vop(lambda q, col_i=col_i: q.reciprocal(
                out=swc[:, col_i:col_i + 1], in_=mwc[:, col_i:col_i + 1]),
                reads=[RSC], writes=[RSC])

        def w_quant(w_ap, scratch, rows, cols, col_i, dmaq):
            scr_d = scratch.rearrange("k d o -> d k o")
            npc = cols // D
            for r in range(rows // 128):
                for c in range(npc):
                    i = r * npc + c
                    stf = wstg[:, (i % 2) * D:(i % 2 + 1) * D]
                    kb.dma(dmaq, stf,
                           w_ap[r * 128:(r + 1) * 128, c * D:(c + 1) * D],
                           reads=[RWS[i % 2]], writes=[RWS[i % 2]], lane=i % 2)
                    vop(lambda q, stf=stf, col_i=col_i: q.tensor_scalar(
                        out=qstg[:], in0=stf, scalar1=swc[:, col_i:col_i + 1],
                        scalar2=1.0, op0=ALU.mult, op1=ALU.min),
                        reads=[RWS[i % 2], RSC], writes=[RQS])
                    vop(lambda q: q.tensor_scalar(
                        out=qstg2[:], in0=qstg[:], scalar1=-1.0, scalar2=1536.0,
                        op0=ALU.max, op1=ALU.add),
                        reads=[RQS], writes=[RQ2])
                    vop(lambda q: q.tensor_scalar(
                        out=qbs[:], in0=qstg2[:], scalar1=-1536.0, scalar2=None,
                        op0=ALU.add),
                        reads=[RQ2], writes=[RQB])
                    ts = tstg[:, (i % 2) * D:(i % 2 + 1) * D]
                    for g in range(4):
                        bank = 4 + (g % 2)
                        fns = []
                        for j in range(4):
                            blk = g * 4 + j
                            fns.append(lambda q, blk=blk, j=j, bank=bank:
                                       q.transpose(out=pbb[bank][:, j * 128:(j + 1) * 128],
                                                   in_=qbs[:, blk * 128:(blk + 1) * 128],
                                                   identity=eyeb[:]))
                        kb.tgroup(fns, reads=[RQB, RROW], writes=[RPB[bank]])
                        aop(lambda q, g=g, bank=bank, ts=ts: q.copy(
                            out=ts[:, g * 512:(g + 1) * 512], in_=pbb[bank][:, 0:512]),
                            reads=[RPB[bank]], writes=[RTS[i % 2]])
                    kb.dma(dmaq,
                           scr_d[:, c * 16:(c + 1) * 16, r * 128:(r + 1) * 128],
                           ts.rearrange("p (k o) -> p k o", k=16),
                           reads=[RTS[i % 2]], writes=[RWD], lane=i % 2)

        def w_bf16_copy(w_ap, scratch, fold_nw, dmaq):
            cols = w_ap.shape[1]
            width = min(cols, D)
            for k in range(KB1):
                i = k
                stf = wstg[:, (i % 2) * D:(i % 2) * D + width]
                kb.dma(dmaq, stf, w_ap[k * 128:(k + 1) * 128, :],
                       reads=[RWS[i % 2]], writes=[RWS[i % 2]], lane=i % 2)
                tb = tstg[:, (i % 2) * D:(i % 2) * D + width]
                if fold_nw:
                    vop(lambda q, stf=stf, tb=tb, k=k: q.tensor_scalar(
                        out=tb, in0=stf, scalar1=nwcol[:, k:k + 1],
                        scalar2=None, op0=ALU.mult),
                        reads=[RWS[i % 2], RROW], writes=[RTS[i % 2]])
                else:
                    vop(lambda q, stf=stf, tb=tb: q.tensor_copy(out=tb, in_=stf),
                        reads=[RWS[i % 2]], writes=[RTS[i % 2]])
                kb.dma(dmaq, scratch[k, :, :], tb,
                       reads=[RTS[i % 2]], writes=[RWD], lane=i % 2)

        w_scale(w1_in, DI, D, 0, "sync")
        w_scale(w2_in, D, DI, 1, "scalar")
        w_quant(w1_in, w1qT, DI, D, 0, "sync")
        w_quant(w2_in, w2qT, D, DI, 1, "sync")
        w_bf16_copy(wq_in, wqb, False, "scalar")
        w_bf16_copy(wk_in, wkb, False, "scalar")
        w_bf16_copy(wv_in, wvb, False, "scalar")
        w_bf16_copy(wo_in, wob, True, "scalar")
        w_bf16_copy(wg1_in, wg1b, False, "scalar")
        A_.pop()

        kb.barrier()

        def tc16(i):
            return tmpc[:, i * 16:(i + 1) * 16]

        def ffn_scalar_prep():
            vop(lambda q: q.tensor_scalar(out=tc16(0), in0=st_sum[:],
                                          scalar1=1.0 / D, scalar2=None, op0=ALU.mult),
                reads=[RST], writes=[RSC])
            vop(lambda q: q.tensor_scalar(out=tc16(1), in0=st_sq[:],
                                          scalar1=1.0 / D, scalar2=None, op0=ALU.mult),
                reads=[RST], writes=[RSC])
            vop(lambda q: q.tensor_tensor(out=tc16(2), in0=tc16(0), in1=tc16(0),
                                          op=ALU.mult),
                reads=[RSC], writes=[RSC])
            vop(lambda q: q.tensor_tensor(out=tc16(3), in0=tc16(1), in1=tc16(2),
                                          op=ALU.subtract),
                reads=[RSC], writes=[RSC])
            vop(lambda q: q.tensor_scalar(out=tc16(2), in0=tc16(3),
                                          scalar1=1.0 + EPS, scalar2=EPS * EPS,
                                          op0=ALU.mult, op1=ALU.add),
                reads=[RSC], writes=[RSC])
            vop(lambda q: q.reciprocal(out=tc16(4), in_=tc16(2)),
                reads=[RSC], writes=[RSC])
            aop(lambda q: q.activation(out=tc16(5), in_=tc16(4), func=AF.Sqrt),
                reads=[RSC], writes=[RSC])
            vop(lambda q: q.tensor_tensor(out=tc16(2), in0=st_mx[:], in1=tc16(0),
                                          op=ALU.subtract),
                reads=[RST, RSC], writes=[RSC])
            vop(lambda q: q.tensor_tensor(out=tc16(1), in0=tc16(0), in1=st_mn[:],
                                          op=ALU.subtract),
                reads=[RST, RSC], writes=[RSC])
            vop(lambda q: q.tensor_tensor(out=tc16(3), in0=tc16(2), in1=tc16(1),
                                          op=ALU.max),
                reads=[RSC], writes=[RSC])
            vop(lambda q: q.tensor_tensor(out=tc16(2), in0=tc16(5), in1=tc16(3),
                                          op=ALU.mult),
                reads=[RSC], writes=[RSC])
            vop(lambda q: q.tensor_scalar(out=tc16(1), in0=tc16(2), scalar1=1e-5,
                                          scalar2=None, op0=ALU.max),
                reads=[RSC], writes=[RSC])
            vop(lambda q: q.reciprocal(out=tc16(2), in_=tc16(1)),
                reads=[RSC], writes=[RSC])
            vop(lambda q: q.tensor_scalar(out=tc16(4), in0=tc16(2), scalar1=127.0,
                                          scalar2=None, op0=ALU.mult),
                reads=[RSC], writes=[RSC])
            vop(lambda q: q.tensor_tensor(out=cA[:], in0=tc16(5), in1=tc16(4),
                                          op=ALU.mult),
                reads=[RSC], writes=[RSC])
            vop(lambda q: q.tensor_tensor(out=tc16(2), in0=tc16(0), in1=cA[:],
                                          op=ALU.mult),
                reads=[RSC], writes=[RSC])
            vop(lambda q: q.tensor_scalar(out=cB[:], in0=tc16(2), scalar1=-1.0,
                                          scalar2=1536.0, op0=ALU.mult, op1=ALU.add),
                reads=[RSC], writes=[RSC])
            vop(lambda q: q.tensor_scalar(out=u1f[:], in0=tc16(1),
                                          scalar1=mwc[:, 0:1], scalar2=1.0 / 127.0,
                                          op0=ALU.mult, op1=ALU.mult),
                reads=[RSC], writes=[RSC])
            vop(lambda q: q.tensor_scalar(out=cUh[:], in0=u1f[:], scalar1=0.5,
                                          scalar2=None, op0=ALU.mult),
                reads=[RSC], writes=[RSC])
            vop(lambda q: q.reciprocal(out=tc16(2), in_=u1f[:]),
                reads=[RSC], writes=[RSC])
            cols_to_rows(tc16(2), 16, uv1d, uv1r[0:1, :], RSC)

        def attn_scalar_prep():
            vop(lambda q: q.tensor_scalar(out=tc16(0), in0=st_sum[:],
                                          scalar1=1.0 / D, scalar2=None, op0=ALU.mult),
                reads=[RST], writes=[RSC])
            vop(lambda q: q.tensor_scalar(out=tc16(1), in0=st_sq[:],
                                          scalar1=1.0 / D, scalar2=None, op0=ALU.mult),
                reads=[RST], writes=[RSC])
            vop(lambda q: q.tensor_tensor(out=tc16(2), in0=tc16(0), in1=tc16(0),
                                          op=ALU.mult),
                reads=[RSC], writes=[RSC])
            vop(lambda q: q.tensor_tensor(out=tc16(3), in0=tc16(1), in1=tc16(2),
                                          op=ALU.subtract),
                reads=[RSC], writes=[RSC])
            vop(lambda q: q.tensor_scalar(out=tc16(2), in0=tc16(3), scalar1=EPS,
                                          scalar2=None, op0=ALU.add),
                reads=[RSC], writes=[RSC])
            vop(lambda q: q.reciprocal(out=tc16(4), in_=tc16(2)),
                reads=[RSC], writes=[RSC])
            aop(lambda q: q.activation(out=cMu[:], in_=tc16(4), func=AF.Sqrt),
                reads=[RSC], writes=[RSC])
            vop(lambda q: q.tensor_tensor(out=tc16(2), in0=tc16(0), in1=cMu[:],
                                          op=ALU.mult),
                reads=[RSC], writes=[RSC])
            vop(lambda q: q.tensor_scalar(out=cMb[:], in0=tc16(2), scalar1=-1.0,
                                          scalar2=None, op0=ALU.mult),
                reads=[RSC], writes=[RSC])

        def emit_ffn(call):
            A_.push()
            R0 = A_.alloc(128, GTC * DI, BF16)
            R0b = R0.bitcast(I8)
            gbc = A_.alloc(128, DI, BF16)
            bbc = A_.alloc(128, DI, BF16)
            WT = A_.alloc(128, 2 * 2048, BF16)
            rofm = A_.alloc(128, 2 * 512, F16)
            xq8r = A_.alloc(128, 512, I8)
            XQU = A_.alloc(128, GTC * D, BF16)
            rtan = A_.alloc(128, 2 * 512, BF16)
            rtp = A_.alloc(128, 512, BF16)
            rh1 = A_.alloc(128, 512, BF16)
            rt1 = rh1
            rz = rtan[:, 0:512]
            rsq = rtan[:, 0:512]
            rof = A_.alloc(128, 512, F32)
            RH = [Res(f"h{i}") for i in range(GTC)]
            RGB = Res("gbc")
            RWT = [Res("wt0"), Res("wt1")]
            RRm = [Res("rofm0"), Res("rofm1")]
            RX8 = Res("xq8r")
            RRt2 = [Res("rtan0"), Res("rtan1")]
            RRt = RRt2[0]
            RXQ = [Res(f"xqu{i}") for i in range(GTC)]
            RRp = Res("rtp")
            RR1 = Res("rh1")
            RRt1 = RR1
            RRz = RRt
            RRsq = RRt
            RRo = [Res("rof")]
            RRo = [RRo[0], RRo[0]]
            RB1S = [Res("b1s0"), Res("b1s1")]
            RB2S = [Res("b2s0"), Res("b2s1")]

            kb.dma("sync", gbc[:],
                   g_bf.rearrange("(a b) -> a b", a=1).broadcast_to([128, DI]),
                   reads=[RROW], writes=[RGB], lane=2)
            kb.dma("sync", bbc[:],
                   b_bf.rearrange("(a b) -> a b", a=1).broadcast_to([128, DI]),
                   reads=[RROW], writes=[RGB], lane=2)

            ffn_scalar_prep()

            for g in range(NG):
                for t4 in range(GTC):
                    t = g * GTC + t4
                    for sl in range(4):
                        xsl = X[:, t * D + sl * 512:t * D + (sl + 1) * 512]
                        pr = sl % 2
                        rofms = rofm[:, pr * 512:(pr + 1) * 512]
                        rtans = rtan[:, pr * 512:(pr + 1) * 512]
                        aop(lambda q, xsl=xsl, t=t, rofms=rofms: q.activation(
                            out=rofms, in_=xsl, func=AF.Identity,
                            scale=cA[:, t:t + 1], bias=cB[:, t:t + 1]),
                            reads=[RX[t], RSC], writes=[RRm[pr]])
                        vop(lambda q, rofms=rofms, rtans=rtans: q.tensor_scalar(
                            out=rtans, in0=rofms, scalar1=-1536.0,
                            scalar2=None, op0=ALU.add),
                            reads=[RRm[pr]], writes=[RRt2[pr]])
                        bank = sl % 2
                        fns = []
                        for j in range(4):
                            fns.append(lambda q, j=j, bank=bank, rtans=rtans:
                                       q.transpose(
                                           out=pbb[bank][:, j * 128:(j + 1) * 128],
                                           in_=rtans[:, j * 128:(j + 1) * 128],
                                           identity=eyeb[:]))
                        kb.tgroup(fns, reads=[RRt2[pr], RROW], writes=[RPB[bank]])
                        vop(lambda q, bank=bank: q.tensor_copy(
                            out=xq8r[:], in_=pbb[bank][:, 0:512]),
                            reads=[RPB[bank]], writes=[RX8])
                        vop(lambda q, t4=t4, sl=sl: q.tensor_copy(
                            out=XQU[:, t4 * D + sl * 512:t4 * D + (sl + 1) * 512],
                            in_=xq8r[:]),
                            reads=[RX8], writes=[RXQ[t4]])

                if SUB < 2:
                    continue
                for os in range(16):
                    kb.dma("sync", b1stg[0:1, (os % 2) * 512:(os % 2 + 1) * 512],
                           b1_bf[os * 512:(os + 1) * 512]
                           .rearrange("(a b) -> a b", a=1),
                           reads=[RB1S[os % 2]], writes=[RB1S[os % 2]], lane=2)
                    for kq in range(4):
                        buf = kq % 2
                        kb.dma("sync",
                               WT[:, buf * 2048:(buf + 1) * 2048]
                               .rearrange("p (k o) -> p k o", k=4),
                               w1qT[kq * 4:(kq + 1) * 4, :, os * 512:(os + 1) * 512]
                               .rearrange("k d o -> d k o"),
                               reads=[RWD], writes=[RWT[buf]], lane=buf)
                        for t4 in range(GTC):
                            bank = 4 * (os % 2) + t4
                            fns = []
                            for j in range(4):
                                k = kq * 4 + j
                                fns.append(lambda q, k=k, j=j, buf=buf, t4=t4,
                                           bank=bank:
                                           q.matmul(out=pb[bank][:],
                                                    lhsT=XQU[:, t4 * D + k * 128:
                                                             t4 * D + (k + 1) * 128],
                                                    rhs=WT[:, buf * 2048 + j * 512:
                                                           buf * 2048 + (j + 1) * 512],
                                                    start=(k == 0), stop=False))
                            kb.tgroup(fns, reads=[RXQ[t4], RWT[buf]],
                                      writes=[RPB[bank]])
                    for t4 in range(GTC):
                        t = g * GTC + t4
                        bank = 4 * (os % 2) + t4
                        kb.tgroup([lambda q, t=t, os=os, t4=t4, bank=bank: q.matmul(
                            out=pb[bank][:], lhsT=uv1r[0:1, t * 128:(t + 1) * 128],
                            rhs=b1stg[0:1, (os % 2) * 512:(os % 2 + 1) * 512],
                            start=False, stop=True)],
                            reads=[RROW, RB1S[os % 2]], writes=[RPB[bank]])
                        aop(lambda q, bank=bank, t=t: q.activation(
                            out=rtan[:, 0:512], in_=pb[bank][:], func=AF.Tanh,
                            scale=cUh[:, t:t + 1]),
                            reads=[RPB[bank], RSC], writes=[RRt])
                        vop(lambda q: q.tensor_scalar(
                            out=rtp[:], in0=rtan[:, 0:512], scalar1=1.0,
                            scalar2=None, op0=ALU.add),
                            reads=[RRt], writes=[RRp])
                        vop(lambda q, bank=bank, t=t: q.scalar_tensor_tensor(
                            out=rh1[:], in0=pb[bank][:], scalar=cUh[:, t:t + 1],
                            in1=rtp[:], op0=ALU.mult, op1=ALU.mult),
                            reads=[RPB[bank], RRp, RSC], writes=[RR1])
                        hsl = R0[:, t4 * DI + os * 512:t4 * DI + (os + 1) * 512]
                        aop(lambda q, hsl=hsl, t4=t4, os=os: q.activation(
                            out=hsl, in_=rh1[:], func=AF.Gelu,
                            accum_out=gsum[:, t4 * 16 + os:t4 * 16 + os + 1]),
                            reads=[RR1], writes=[RH[t4], RTMP])
                        aop(lambda q, hsl=hsl, t4=t4, os=os: q.activation(
                            out=rtan[:, 0:512], in_=hsl, func=AF.Square,
                            accum_out=gsq[:, t4 * 16 + os:t4 * 16 + os + 1]),
                            reads=[RH[t4]], writes=[RRt, RTMP])

                def c4(i):
                    return tmpc[:, 64 + i * 4:64 + (i + 1) * 4]
                for t4 in range(GTC):
                    vop(lambda q, t4=t4: q.tensor_reduce(
                        out=c4(0)[:, t4:t4 + 1], in_=gsum[:, t4 * 16:(t4 + 1) * 16],
                        axis=AX.X, op=ALU.add),
                        reads=[RTMP], writes=[RSC])
                    vop(lambda q, t4=t4: q.tensor_reduce(
                        out=c4(1)[:, t4:t4 + 1], in_=gsq[:, t4 * 16:(t4 + 1) * 16],
                        axis=AX.X, op=ALU.add),
                        reads=[RTMP], writes=[RSC])
                vop(lambda q: q.tensor_scalar(out=c4(2), in0=c4(0),
                                              scalar1=1.0 / DI, scalar2=None,
                                              op0=ALU.mult),
                    reads=[RSC], writes=[RSC])
                vop(lambda q: q.tensor_scalar(out=c4(3), in0=c4(1),
                                              scalar1=1.0 / DI, scalar2=None,
                                              op0=ALU.mult),
                    reads=[RSC], writes=[RSC])
                vop(lambda q: q.tensor_tensor(out=c4(0), in0=c4(2), in1=c4(2),
                                              op=ALU.mult),
                    reads=[RSC], writes=[RSC])
                vop(lambda q: q.tensor_tensor(out=c4(1), in0=c4(3), in1=c4(0),
                                              op=ALU.subtract),
                    reads=[RSC], writes=[RSC])
                vop(lambda q: q.tensor_scalar(out=c4(0), in0=c4(1), scalar1=EPS,
                                              scalar2=None, op0=ALU.add),
                    reads=[RSC], writes=[RSC])
                vop(lambda q: q.reciprocal(out=c4(1), in_=c4(0)),
                    reads=[RSC], writes=[RSC])
                aop(lambda q: q.activation(out=c2A[:], in_=c4(1), func=AF.Sqrt),
                    reads=[RSC], writes=[RSC])
                vop(lambda q: q.tensor_tensor(out=c4(0), in0=c4(2), in1=c2A[:],
                                              op=ALU.mult),
                    reads=[RSC], writes=[RSC])
                vop(lambda q: q.tensor_scalar(out=c2B[:], in0=c4(0), scalar1=-1.0,
                                              scalar2=None, op0=ALU.mult),
                    reads=[RSC], writes=[RSC])

                for t4 in range(GTC):
                    t = g * GTC + t4
                    for sl in range(16):
                        hsl = R0[:, t4 * DI + sl * 512:t4 * DI + (sl + 1) * 512]
                        aop(lambda q, hsl=hsl, t4=t4: q.activation(
                            out=rt1[:], in_=hsl, func=AF.Identity,
                            scale=c2A[:, t4:t4 + 1], bias=c2B[:, t4:t4 + 1]),
                            reads=[RH[t4], RSC], writes=[RRt1])
                        vop(lambda q, sl=sl: q.tensor_tensor(
                            out=rz[:], in0=rt1[:],
                            in1=gbc[:, sl * 512:(sl + 1) * 512], op=ALU.mult),
                            reads=[RRt1, RGB], writes=[RRz])
                        vop(lambda q, hsl=hsl, sl=sl, t4=t4: q.tensor_tensor(
                            out=hsl, in0=rz[:],
                            in1=bbc[:, sl * 512:(sl + 1) * 512], op=ALU.add),
                            reads=[RRz, RGB], writes=[RH[t4]])
                        vop(lambda q, hsl=hsl, sl=sl: q.tensor_reduce(
                            out=am2[:, sl:sl + 1], in_=hsl, axis=AX.X, op=ALU.max,
                            apply_absolute_value=True),
                            reads=[RH[t4]], writes=[RTMP])
                        aop(lambda q, hsl=hsl, sl=sl: q.activation(
                            out=rsq[:], in_=hsl, func=AF.Square,
                            accum_out=ms2[:, sl:sl + 1]),
                            reads=[RH[t4]], writes=[RRsq, RTMP])
                    vop(lambda q: q.tensor_reduce(out=tmpc[:, 0:1], in_=am2[:],
                                                  axis=AX.X, op=ALU.max),
                        reads=[RTMP], writes=[RSC])
                    vop(lambda q: q.tensor_reduce(out=tmpc[:, 1:2], in_=ms2[:],
                                                  axis=AX.X, op=ALU.add),
                        reads=[RTMP], writes=[RSC])
                    vop(lambda q: q.tensor_scalar(out=tmpc[:, 2:3], in0=tmpc[:, 0:1],
                                                  scalar1=1e-5, scalar2=None,
                                                  op0=ALU.max),
                        reads=[RSC], writes=[RSC])
                    vop(lambda q: q.reciprocal(out=tmpc[:, 3:4], in_=tmpc[:, 2:3]),
                        reads=[RSC], writes=[RSC])
                    vop(lambda q, t4=t4: q.tensor_scalar(
                        out=cS2[:, t4:t4 + 1], in0=tmpc[:, 3:4], scalar1=127.0,
                        scalar2=None, op0=ALU.mult),
                        reads=[RSC], writes=[RSC])
                    vop(lambda q: q.tensor_scalar(out=tmpc[:, 4:5], in0=tmpc[:, 1:2],
                                                  scalar1=1.0 / DI, scalar2=EPS,
                                                  op0=ALU.mult, op1=ALU.add),
                        reads=[RSC], writes=[RSC])
                    vop(lambda q: q.reciprocal(out=tmpc[:, 5:6], in_=tmpc[:, 4:5]),
                        reads=[RSC], writes=[RSC])
                    aop(lambda q: q.activation(out=tmpc[:, 6:7], in_=tmpc[:, 5:6],
                                               func=AF.Sqrt),
                        reads=[RSC], writes=[RSC])
                    vop(lambda q: q.tensor_tensor(out=tmpc[:, 7:8], in0=tmpc[:, 6:7],
                                                  in1=tmpc[:, 0:1], op=ALU.mult),
                        reads=[RSC], writes=[RSC])
                    vop(lambda q: q.tensor_scalar(out=tmpc[:, 8:9], in0=tmpc[:, 7:8],
                                                  scalar1=1e-5, scalar2=None,
                                                  op0=ALU.max),
                        reads=[RSC], writes=[RSC])
                    vop(lambda q, t4=t4: q.tensor_scalar(
                        out=cU2[:, t4:t4 + 1], in0=tmpc[:, 8:9],
                        scalar1=mwc[:, 1:2], scalar2=0.5 / 127.0,
                        op0=ALU.mult, op1=ALU.mult),
                        reads=[RSC], writes=[RSC])
                vop(lambda q: q.reciprocal(out=uvt[:, 0:4], in_=cU2[:]),
                    reads=[RSC], writes=[RTMP])
                cols_to_rows(uvt[:, 0:4], 4, uv2d, uv2r[0:1, :], RTMP)

                if SUB < 5:
                    continue
                for t4 in range(GTC):
                    for sl in range(16):
                        hsl = R0[:, t4 * DI + sl * 512:t4 * DI + (sl + 1) * 512]
                        pr = sl % 2
                        rofms = rofm[:, pr * 512:(pr + 1) * 512]
                        rtans = rtan[:, pr * 512:(pr + 1) * 512]
                        aop(lambda q, hsl=hsl, t4=t4, rofms=rofms: q.activation(
                            out=rofms, in_=hsl, func=AF.Identity,
                            scale=cS2[:, t4:t4 + 1], bias=halfc[:]),
                            reads=[RH[t4], RSC], writes=[RRm[pr]])
                        vop(lambda q, rofms=rofms, rtans=rtans: q.tensor_scalar(
                            out=rtans, in0=rofms, scalar1=-1536.0,
                            scalar2=None, op0=ALU.add),
                            reads=[RRm[pr]], writes=[RRt2[pr]])
                        bank = sl % 2
                        if SUB >= 6:
                            fns = []
                            for j in range(4):
                                fns.append(lambda q, j=j, bank=bank, rtans=rtans:
                                           q.transpose(
                                               out=pbb[bank][:, j * 128:(j + 1) * 128],
                                               in_=rtans[:, j * 128:(j + 1) * 128],
                                               identity=eyeb[:]))
                            kb.tgroup(fns, reads=[RRt2[pr], RROW], writes=[RPB[bank]])
                        else:
                            continue
                        dst8 = R0b[:, (t4 * DI + sl * 512) * 2:
                                   (t4 * DI + sl * 512) * 2 + 512]
                        if SUB >= 8 and (g < QG or QG == 9):
                            aop(lambda q, bank=bank, dst8=dst8: q.copy(
                                out=dst8, in_=pbb[bank][:, 0:512]),
                                reads=[RPB[bank]], writes=[RH[t4]])
                        elif SUB >= 7 and (g >= 4 - QG if QG <= 4 else True):
                            aop(lambda q, bank=bank: q.copy(
                                out=xq8r[:], in_=pbb[bank][:, 0:512]),
                                reads=[RPB[bank], RX8], writes=[RX8])

                if GBAR:
                    kb.barrier()
                if SUB < 9:
                    continue
                for t4 in range(GTC):
                    t = g * GTC + t4
                    src8 = R0b[:, t4 * DI * 2:(t4 + 1) * DI * 2] \
                        .rearrange("p (s b) -> p s b", b=1024)[:, :, 0:512]
                    dst3 = XQU[:, 0:DI].rearrange("p (s b) -> p s b", b=512)
                    aop(lambda q, src8=src8, dst3=dst3: q.copy(
                        out=dst3, in_=src8),
                        reads=[RH[t4]], writes=RXQ)
                    for o2s in range(4):
                        bk2 = 4 + 2 * (t4 % 2) + o2s % 2
                        kb.dma("sync",
                               b2stg[0:1, (o2s % 2) * 512:(o2s % 2 + 1) * 512],
                               b2h_bf[o2s * 512:(o2s + 1) * 512]
                               .rearrange("(a b) -> a b", a=1),
                               reads=[RB2S[o2s % 2]], writes=[RB2S[o2s % 2]],
                               lane=2)
                        for kq in range(16):
                            buf = kq % 2
                            kb.dma("sync",
                                   WT[:, buf * 2048:(buf + 1) * 2048]
                                   .rearrange("p (k o) -> p k o", k=4),
                                   w2qT[kq * 4:(kq + 1) * 4, :,
                                        o2s * 512:(o2s + 1) * 512]
                                   .rearrange("k d o -> d k o"),
                                   reads=[RWD], writes=[RWT[buf]], lane=buf)
                            fns = []
                            fns = []
                            for j in range(4):
                                k = kq * 4 + j
                                fns.append(lambda q, k=k, j=j, buf=buf, bk2=bk2:
                                           q.matmul(out=pb[bk2][:],
                                                    lhsT=XQU[:, k * 128:(k + 1) * 128],
                                                    rhs=WT[:, buf * 2048 + j * 512:
                                                           buf * 2048 + (j + 1) * 512],
                                                    start=(k == 0), stop=False))
                            kb.tgroup(fns, reads=[RXQ[0], RWT[buf]],
                                      writes=[RPB[bk2]])
                        kb.tgroup([lambda q, t4=t4, o2s=o2s, bk2=bk2: q.matmul(
                            out=pb[bk2][:],
                            lhsT=uv2r[0:1, t4 * 128:(t4 + 1) * 128],
                            rhs=b2stg[0:1, (o2s % 2) * 512:(o2s % 2 + 1) * 512],
                            start=False, stop=True)],
                            reads=[RROW, RB2S[o2s % 2]], writes=[RPB[bk2]])
                        xsl = X[:, t * D + o2s * 512:t * D + (o2s + 1) * 512]
                        ro = rof[:, 0:512]
                        vop(lambda q, ro=ro, xsl=xsl, t=t, t4=t4, bk2=bk2:
                            q.scalar_tensor_tensor(
                                out=ro, in0=pb[bk2][:], scalar=cU2[:, t4:t4 + 1],
                                in1=xsl, op0=ALU.mult, op1=ALU.add),
                            reads=[RPB[bk2], RX[t], RSC], writes=[RRo[0]])
                        if call == 0:
                            aop(lambda q, ro=ro, xsl=xsl: q.copy(
                                out=xsl, in_=ro),
                                reads=[RRo[0]], writes=[RX[t]])
                        else:
                            kb.dma("gpsimd",
                                   out_t[t * 128:(t + 1) * 128,
                                         o2s * 512:(o2s + 1) * 512],
                                   ro, reads=[RRo[0]], writes=[RRo[0]],
                                   lane=t4 % 2)
                if GBAR:
                    kb.barrier()
                if call == 0:
                    for t4 in range(GTC):
                        t = g * GTC + t4
                        xc = X[:, t * D:(t + 1) * D]
                        vop(lambda q, xc=xc, t=t: q.tensor_reduce(
                            out=st_sum[:, t:t + 1], in_=xc, axis=AX.X, op=ALU.add),
                            reads=[RX[t]], writes=[RST])
                        for sl in range(4):
                            aop(lambda q, xc=xc, sl=sl: q.activation(
                                out=rsq[:], in_=xc[:, sl * 512:(sl + 1) * 512],
                                func=AF.Square,
                                accum_out=tmpc[:, 24 + sl:25 + sl]),
                                reads=[RX[t]], writes=[RRsq, RTMP])
                        vop(lambda q, t=t: q.tensor_reduce(
                            out=st_sq[:, t:t + 1], in_=tmpc[:, 24:28],
                            axis=AX.X, op=ALU.add),
                            reads=[RTMP], writes=[RST])
            A_.pop()

        def emit_attn():
            A_.push()
            S0 = A_.alloc(128, H * HD, F32)
            S1 = A_.alloc(128, H * HD, F32)
            Dt = A_.alloc(128, H * 128, BF16)
            KDt = A_.alloc(128, H * HD, BF16)
            yT = A_.alloc(128, KB1 * 512, BF16)
            QT = A_.alloc(128, H * 512, BF16)
            onT = A_.alloc(128, KB1 * 512, BF16)
            WT = A_.alloc(128, 2 * 2048, BF16)
            KG = A_.alloc(128, GA * D, BF16)
            VG = A_.alloc(128, GA * D, BF16)
            KH = A_.alloc(128, D, BF16)
            Sbf = A_.alloc(128, 512, BF16)
            KTr = A_.alloc(128, 512, BF16)
            sD = KTr
            oB = A_.alloc(128, 512, BF16)
            oC = A_.alloc(128, D, BF16)
            og = A_.alloc(128, D, BF16)
            sg = A_.alloc(128, D, BF16)
            WG2 = A_.alloc(128, D, BF16)
            g1T = A_.alloc(128, GTC * 128, BF16)
            roa = A_.alloc(128, 512, F32)
            g1r = oB[:, 0:128]
            ybr = og
            dscr = roa[:, 0:128]
            RYT = Res("yT")
            RQT = Res("QT")
            RONT = Res("onT")
            RWT = [Res("awt0"), Res("awt1")]
            RKG = [Res(f"kg{i}") for i in range(GA)]
            RVG = [Res(f"vg{i}") for i in range(GA)]
            RKH = Res("kh")
            RSBF = Res("sbf")
            RKT = Res("ktr")
            RSD = RKT
            ROB = Res("ob")
            ROC = Res("oc")
            RSG = Res("sg")
            RWG2 = Res("wg2")
            RG1T = Res("g1t")
            RRo = [Res("roa")]
            RRo = [RRo[0], RRo[0]]
            RG1 = Res("obg1")
            RYB = Res("og_ybr")
            RDS = RRo[0]

            attn_scalar_prep()

            for h in range(H):
                vop(lambda q, h=h: q.tensor_scalar(
                    out=dscr[:], in0=difft[:], scalar1=ldbc[:, h:h + 1],
                    scalar2=None, op0=ALU.mult),
                    reads=[RROW, RDEC], writes=[RDS])
                aop(lambda q, h=h: q.activation(
                    out=Dt[:, h * 128:(h + 1) * 128], in_=dscr[:], func=AF.Exp),
                    reads=[RDS], writes=[RDEC])
            vop(lambda q: q.tensor_copy(
                out=KDt[:].rearrange("p (h e) -> p h e", h=H),
                in_=kdt[:].unsqueeze(2).broadcast_to([128, H, HD])),
                reads=[RDEC], writes=[RDEC])
            vop(lambda q: q.memset(S0[:], 0.0), writes=[RS_[0]])

            def build_y(t):
                tn = t % GTC if True else 0
                aop(lambda q, t=t: q.activation(
                    out=ybr[:], in_=X[:, t * D:(t + 1) * D], func=AF.Identity,
                    scale=cMu[:, t:t + 1], bias=cMb[:, t:t + 1]),
                    reads=[RX[t], RSC], writes=[RYB])
                for gblk in range(4):
                    bank = gblk % 2
                    fns = []
                    for j in range(4):
                        k = gblk * 4 + j
                        fns.append(lambda q, k=k, j=j, bank=bank:
                                   q.transpose(out=pbb[bank][:, j * 128:(j + 1) * 128],
                                               in_=ybr[:, k * 128:(k + 1) * 128],
                                               identity=eyeb[:]))
                    kb.tgroup(fns, reads=[RYB, RROW], writes=[RPB[bank]])
                    for j in range(4):
                        k = gblk * 4 + j
                        vop(lambda q, k=k, j=j, bank=bank, tn=tn: q.tensor_copy(
                            out=yT[:, k * 512 + tn * 128:k * 512 + (tn + 1) * 128],
                            in_=pbb[bank][:, j * 128:(j + 1) * 128]),
                            reads=[RPB[bank]], writes=[RYT])

            def stream_w_tiles(wsrc, os, width, dmaq="sync"):
                for kq in range(4):
                    buf = kq % 2
                    kb.dma(dmaq,
                           WT[:, buf * 2048:buf * 2048 + 4 * width]
                           .rearrange("p (k o) -> p k o", k=4),
                           wsrc[kq * 4:(kq + 1) * 4, :, os * width:(os + 1) * width]
                           .rearrange("k d o -> d k o"),
                           reads=[RWD], writes=[RWT[buf]], lane=buf)

            for g in range(NTC // GA):
                for t2 in range(GA):
                    build_y(g * GA + t2)
                for (wsrc, dscr_t, RDST, DST) in ((wkb, kscr, RKG, KG),
                                                  (wvb, vscr, RVG, VG)):
                    for os in range(4):
                        for kq in range(4):
                            buf = kq % 2
                            kb.dma("sync",
                                   WT[:, buf * 2048:(buf + 1) * 2048]
                                   .rearrange("p (k o) -> p k o", k=4),
                                   wsrc[kq * 4:(kq + 1) * 4, :,
                                        os * 512:(os + 1) * 512]
                                   .rearrange("k d o -> d k o"),
                                   reads=[RWD], writes=[RWT[buf]], lane=buf)
                            for t2 in range(GA):
                                t = g * GA + t2
                                tn = t % GTC
                                bank = 4 + t2 % 2
                                fns = []
                                for j in range(4):
                                    k = kq * 4 + j
                                    fns.append(lambda q, k=k, j=j, buf=buf,
                                               bank=bank, tn=tn:
                                               q.matmul(out=pb[bank][:],
                                                        lhsT=yT[:, k * 512 + tn * 128:
                                                                k * 512 + (tn + 1) * 128],
                                                        rhs=WT[:, buf * 2048 + j * 512:
                                                               buf * 2048 + (j + 1) * 512],
                                                        start=(k == 0), stop=(k == 15)))
                                kb.tgroup(fns, reads=[RYT, RWT[buf]],
                                          writes=[RPB[bank]])
                        for t2 in range(GA):
                            t = g * GA + t2
                            bank = 4 + t2 % 2
                            dsl = DST[:, t2 * D + os * 512:t2 * D + (os + 1) * 512]
                            aop(lambda q, bank=bank, dsl=dsl: q.activation(
                                out=dsl, in_=pb[bank][:], func=AF.Silu),
                                reads=[RPB[bank]], writes=[RDST[t2]])
                            kb.dma("gpsimd",
                                   dscr_t[t, :, os * 512:(os + 1) * 512], dsl,
                                   reads=[RDST[t2]], writes=[RWD], lane=t2 % 2)
                for t2 in range(GA):
                    t = g * GA + t2
                    cur, nxt = RS_[t % 2], RS_[(t + 1) % 2]
                    Scur = S0 if t % 2 == 0 else S1
                    Snxt = S1 if t % 2 == 0 else S0
                    vop(lambda q, t2=t2: q.tensor_tensor(
                        out=KH[:], in0=KG[:, t2 * D:(t2 + 1) * D], in1=KDt[:],
                        op=ALU.mult),
                        reads=[RKG[t2], RDEC], writes=[RKH])
                    for hg in range(4):
                        bank = 6 + hg % 2
                        fns = []
                        for hh in range(4):
                            h = hg * 4 + hh
                            fns.append(lambda q, h=h, hh=hh, bank=bank, t2=t2:
                                       q.matmul(out=pb[bank][:, hh * 128:(hh + 1) * 128],
                                                lhsT=KH[:, h * 128:(h + 1) * 128],
                                                rhs=VG[:, t2 * D + h * 128:
                                                        t2 * D + (h + 1) * 128],
                                                start=True, stop=True))
                        kb.tgroup(fns, reads=[RKH, RVG[t2]], writes=[RPB[bank]])
                        for hh in range(4):
                            h = hg * 4 + hh
                            vop(lambda q, h=h, hh=hh, bank=bank, Scur=Scur,
                                Snxt=Snxt: q.scalar_tensor_tensor(
                                    out=Snxt[:, h * 128:(h + 1) * 128],
                                    in0=Scur[:, h * 128:(h + 1) * 128],
                                    scalar=cdt[:, h:h + 1],
                                    in1=pb[bank][:, hh * 128:(hh + 1) * 128],
                                    op0=ALU.mult, op1=ALU.add),
                                reads=[cur, RPB[bank], RDEC], writes=[nxt])

            kb.dma("gpsimd", cc_in_t.ap(), S0[:], reads=[RS_[0]], writes=[RCC], lane=0)
            kb.collective("AllGather", ALU.bypass,
                          [[0, 1], [2, 3], [4, 5], [6, 7]],
                          [cc_in_t.ap().opt()], [cc_out_t.ap().opt()],
                          reads=[RCC], writes=[RCC])
            kb.dma("gpsimd", S1[:], cc_out_t.ap()[0], reads=[RCC, RS_[1]],
                   writes=[RS_[1]], lane=0)
            vop(lambda q: q.tensor_scalar(out=S0[:], in0=S1[:],
                                          scalar1=maskb[:],
                                          scalar2=None, op0=ALU.mult),
                reads=[RS_[1], RROW], writes=[RS_[0]])

            for sl in range(4):
                kb.dma("sync", roa[:, 0:512], wg2_in[:, sl * 512:(sl + 1) * 512],
                       reads=[RRo[0]], writes=[RRo[0]], lane=3)
                vop(lambda q, sl=sl: q.tensor_copy(
                    out=WG2[:, sl * 512:(sl + 1) * 512], in_=roa[:, 0:512]),
                    reads=[RRo[0]], writes=[RWG2])

            for g in range(NG):
                for t4 in range(GTC):
                    build_y(g * GTC + t4)
                for eb in range(16):
                    bank = 4 + eb % 2
                    for kq in range(4):
                        buf = kq % 2
                        kb.dma("sync",
                               WT[:, buf * 2048:buf * 2048 + 512]
                               .rearrange("p (k o) -> p k o", k=4),
                               wqb[kq * 4:(kq + 1) * 4, :, eb * 128:(eb + 1) * 128]
                               .rearrange("k d o -> d k o"),
                               reads=[RWD], writes=[RWT[buf]], lane=buf)
                        fns = []
                        for j in range(4):
                            k = kq * 4 + j
                            fns.append(lambda q, k=k, j=j, buf=buf, bank=bank:
                                       q.matmul(out=pb[bank][:],
                                                lhsT=WT[:, buf * 2048 + j * 128:
                                                        buf * 2048 + (j + 1) * 128],
                                                rhs=yT[:, k * 512:(k + 1) * 512],
                                                start=(k == 0), stop=(k == 15)))
                        kb.tgroup(fns, reads=[RYT, RWT[buf]], writes=[RPB[bank]])
                    aop(lambda q, bank=bank, eb=eb: q.activation(
                        out=QT[:, eb * 512:(eb + 1) * 512], in_=pb[bank][:],
                        func=AF.Silu),
                        reads=[RPB[bank]], writes=[RQT])
                for kq in range(4):
                    buf = kq % 2
                    kb.dma("sync",
                           WT[:, buf * 2048:buf * 2048 + 512]
                           .rearrange("p (k o) -> p k o", k=4),
                           wg1b[kq * 4:(kq + 1) * 4, :, :]
                           .rearrange("k d o -> d k o"),
                           reads=[RWD], writes=[RWT[buf]], lane=buf)
                    for t4 in range(GTC):
                        fns = []
                        for j in range(4):
                            k = kq * 4 + j
                            fns.append(lambda q, k=k, j=j, buf=buf, t4=t4:
                                       q.matmul(out=pb[4 + t4][:, 0:128],
                                                lhsT=yT[:, k * 512 + t4 * 128:
                                                        k * 512 + (t4 + 1) * 128],
                                                rhs=WT[:, buf * 2048 + j * 128:
                                                       buf * 2048 + (j + 1) * 128],
                                                start=(k == 0), stop=(k == 15)))
                        kb.tgroup(fns, reads=[RYT, RWT[buf]], writes=[RPB[4 + t4]])
                for t4 in range(GTC):
                    aop(lambda q, t4=t4: q.copy(out=g1r[:],
                                                in_=pb[4 + t4][:, 0:128]),
                        reads=[RPB[4 + t4]], writes=[ROB])
                    kb.tgroup([lambda q: q.transpose(out=pbb[0][:, 0:128],
                                                     in_=g1r[:], identity=eyeb[:])],
                              reads=[ROB, RROW], writes=[RPB[0]])
                    vop(lambda q, t4=t4: q.tensor_copy(
                        out=g1T[:, t4 * 128:(t4 + 1) * 128], in_=pbb[0][:, 0:128]),
                        reads=[RPB[0]], writes=[RG1T])
                for t4 in range(GTC):
                    t = g * GTC + t4
                    p2 = t4 % 2
                    kb.dma("sync", KG[:, p2 * D:(p2 + 1) * D], kscr[t],
                           reads=[RWD], writes=[RKG[p2]], lane=2 + p2)
                    kb.dma("sync", VG[:, p2 * D:(p2 + 1) * D], vscr[t],
                           reads=[RWD], writes=[RVG[p2]], lane=2 + p2)
                    fns = []
                    for os in range(4):
                        fns.append(lambda q, os=os, t4=t4: q.matmul(
                            out=pb[7][:, 0:512] if False else pb[7][:],
                            lhsT=g1T[:, t4 * 128:(t4 + 1) * 128],
                            rhs=WG2[:, os * 512:(os + 1) * 512],
                            start=True, stop=True) if False else q.matmul(
                            out=pb[7][:],
                            lhsT=g1T[:, t4 * 128:(t4 + 1) * 128],
                            rhs=WG2[:, os * 512:(os + 1) * 512],
                            start=True, stop=True))
                    for os in range(4):
                        kb.tgroup([lambda q, os=os, t4=t4: q.matmul(
                            out=pb[7][:],
                            lhsT=g1T[:, t4 * 128:(t4 + 1) * 128],
                            rhs=WG2[:, os * 512:(os + 1) * 512],
                            start=True, stop=True)],
                            reads=[RG1T, RWG2], writes=[RPB[7]])
                        aop(lambda q, os=os, t4=t4: q.activation(
                            out=sg[:, os * 512:(os + 1) * 512]
                            if False else sg[:, os * 512:(os + 1) * 512],
                            in_=pb[7][:], func=AF.Sigmoid),
                            reads=[RPB[7]], writes=[RSG])
                    cur, nxt = RS_[t % 2], RS_[(t + 1) % 2]
                    Scur = S0 if t % 2 == 0 else S1
                    Snxt = S1 if t % 2 == 0 else S0
                    vop(lambda q, p2=p2: q.tensor_tensor(
                        out=KH[:], in0=KG[:, p2 * D:(p2 + 1) * D], in1=KDt[:],
                        op=ALU.mult),
                        reads=[RKG[p2], RDEC], writes=[RKH])
                    for hg in range(4):
                        fns = []
                        for hh in range(4):
                            h = hg * 4 + hh
                            fns.append(lambda q, h=h, hh=hh, p2=p2: q.transpose(
                                out=pbb[0][:, hh * 128:(hh + 1) * 128],
                                in_=KG[:, p2 * D + h * 128:p2 * D + (h + 1) * 128],
                                identity=eyeb[:]))
                        kb.tgroup(fns, reads=[RKG[p2], RROW], writes=[RPB[0]])
                        vop(lambda q: q.tensor_copy(out=KTr[:], in_=pbb[0][:, 0:512]),
                            reads=[RPB[0]], writes=[RKT])
                        aop(lambda q, hg=hg, Scur=Scur: q.copy(
                            out=Sbf[:], in_=Scur[:, hg * 512:(hg + 1) * 512]),
                            reads=[cur], writes=[RSBF])
                        fns = []
                        for hh in range(4):
                            h = hg * 4 + hh
                            fns.append(lambda q, h=h, hh=hh, t4=t4: q.matmul(
                                out=pb[1][:, hh * 128:(hh + 1) * 128],
                                lhsT=KTr[:, hh * 128:(hh + 1) * 128],
                                rhs=QT[:, h * 512 + t4 * 128:h * 512 + (t4 + 1) * 128],
                                start=True, stop=True))
                        kb.tgroup(fns, reads=[RKT, RQT], writes=[RPB[1]])
                        for hh in range(4):
                            h = hg * 4 + hh
                            vop(lambda q, h=h, hh=hh: q.tensor_tensor(
                                out=sD[:, hh * 128:(hh + 1) * 128],
                                in0=pb[1][:, hh * 128:(hh + 1) * 128],
                                in1=Dt[:, h * 128:(h + 1) * 128], op=ALU.mult),
                                reads=[RPB[1], RDEC], writes=[RSD])
                        fns = []
                        for hh in range(4):
                            h = hg * 4 + hh
                            fns.append(lambda q, h=h, hh=hh, p2=p2: q.matmul(
                                out=pb[2][:, hh * 128:(hh + 1) * 128],
                                lhsT=sD[:, hh * 128:(hh + 1) * 128],
                                rhs=VG[:, p2 * D + h * 128:p2 * D + (h + 1) * 128],
                                start=True, stop=True))
                        kb.tgroup(fns, reads=[RSD, RVG[p2]], writes=[RPB[2]])
                        fns = []
                        for hh in range(4):
                            h = hg * 4 + hh
                            fns.append(lambda q, h=h, hh=hh, t4=t4: q.matmul(
                                out=pb[3][:, hh * 128:(hh + 1) * 128],
                                lhsT=QT[:, h * 512 + t4 * 128:h * 512 + (t4 + 1) * 128],
                                rhs=Sbf[:, hh * 128:(hh + 1) * 128],
                                start=True, stop=True))
                        kb.tgroup(fns, reads=[RQT, RSBF], writes=[RPB[3]])
                        aop(lambda q: q.copy(out=oB[:], in_=pb[2][:]),
                            reads=[RPB[2]], writes=[ROB])
                        for hh in range(4):
                            h = hg * 4 + hh
                            vop(lambda q, h=h, hh=hh, hg=hg: q.scalar_tensor_tensor(
                                out=oC[:, h * 128:(h + 1) * 128],
                                in0=pb[3][:, hh * 128:(hh + 1) * 128],
                                scalar=qdt[:, h:h + 1],
                                in1=oB[:, hh * 128:(hh + 1) * 128],
                                op0=ALU.mult, op1=ALU.add),
                                reads=[RPB[3], ROB, RDEC], writes=[ROC])
                        fns = []
                        for hh in range(4):
                            h = hg * 4 + hh
                            fns.append(lambda q, h=h, hh=hh, p2=p2: q.matmul(
                                out=pb[4][:, hh * 128:(hh + 1) * 128],
                                lhsT=KH[:, h * 128:(h + 1) * 128],
                                rhs=VG[:, p2 * D + h * 128:p2 * D + (h + 1) * 128],
                                start=True, stop=True))
                        kb.tgroup(fns, reads=[RKH, RVG[p2]], writes=[RPB[4]])
                        for hh in range(4):
                            h = hg * 4 + hh
                            vop(lambda q, h=h, hh=hh, Scur=Scur, Snxt=Snxt:
                                q.scalar_tensor_tensor(
                                    out=Snxt[:, h * 128:(h + 1) * 128],
                                    in0=Scur[:, h * 128:(h + 1) * 128],
                                    scalar=cdt[:, h:h + 1],
                                    in1=pb[4][:, hh * 128:(hh + 1) * 128],
                                    op0=ALU.mult, op1=ALU.add),
                                reads=[cur, RPB[4], RDEC], writes=[nxt])
                    if SDBG:
                        kb.dma("gpsimd", sdbg_t[t], Snxt[:], reads=[nxt],
                               writes=[RCC], lane=0)
                    if SDBG:
                        kb.dma("gpsimd", odbg_t[t], oC[:], reads=[ROC],
                               writes=[RCC], lane=0)
                        kb.dma("gpsimd", gdbg_t[t], sg[:], reads=[RSG],
                               writes=[RCC], lane=0)
                    vop(lambda q: q.tensor_tensor(out=og[:], in0=oC[:], in1=sg[:],
                                                  op=ALU.mult),
                        reads=[ROC, RSG], writes=[RYB])
                    vop(lambda q: q.tensor_tensor(out=oC[:], in0=og[:], in1=og[:],
                                                  op=ALU.mult),
                        reads=[RYB], writes=[ROC])
                    vop(lambda q: q.tensor_reduce(
                        out=tmpc[:, 0:16],
                        in_=oC[:].rearrange("p (h e) -> p h e", h=H),
                        axis=AX.X, op=ALU.add),
                        reads=[ROC], writes=[RTMP])
                    vop(lambda q: q.tensor_scalar(out=tmpc[:, 16:32],
                                                  in0=tmpc[:, 0:16],
                                                  scalar1=1.0 / HD, scalar2=EPS,
                                                  op0=ALU.mult, op1=ALU.add),
                        reads=[RTMP], writes=[RTMP])
                    vop(lambda q: q.reciprocal(out=tmpc[:, 32:48],
                                               in_=tmpc[:, 16:32]),
                        reads=[RTMP], writes=[RTMP])
                    aop(lambda q: q.activation(out=tmpc[:, 48:64],
                                               in_=tmpc[:, 32:48], func=AF.Sqrt),
                        reads=[RTMP], writes=[RTMP])
                    vop(lambda q: q.tensor_tensor(
                        out=oC[:].rearrange("p (h e) -> p h e", h=H),
                        in0=og[:].rearrange("p (h e) -> p h e", h=H),
                        in1=tmpc[:, 48:64].unsqueeze(2).broadcast_to([128, H, HD]),
                        op=ALU.mult),
                        reads=[RYB, RTMP], writes=[ROC])
                    if SDBG:
                        kb.dma("gpsimd", ndbg_t[t], oC[:], reads=[ROC],
                               writes=[RCC], lane=0)
                    for gblk in range(4):
                        bank = 5
                        fns = []
                        for j in range(4):
                            k = gblk * 4 + j
                            fns.append(lambda q, k=k, j=j: q.transpose(
                                out=pbb[5][:, j * 128:(j + 1) * 128],
                                in_=oC[:, k * 128:(k + 1) * 128],
                                identity=eyeb[:]))
                        kb.tgroup(fns, reads=[ROC, RROW], writes=[RPB[5]])
                        for j in range(4):
                            k = gblk * 4 + j
                            vop(lambda q, k=k, j=j, t4=t4: q.tensor_copy(
                                out=onT[:, k * 512 + t4 * 128:k * 512 + (t4 + 1) * 128],
                                in_=pbb[5][:, j * 128:(j + 1) * 128]),
                                reads=[RPB[5]], writes=[RONT])
                for os in range(4):
                    for kq in range(4):
                        buf = kq % 2
                        kb.dma("sync",
                               WT[:, buf * 2048:(buf + 1) * 2048]
                               .rearrange("p (k o) -> p k o", k=4),
                               wob[kq * 4:(kq + 1) * 4, :, os * 512:(os + 1) * 512]
                               .rearrange("k d o -> d k o"),
                               reads=[RWD], writes=[RWT[buf]], lane=buf)
                        for t4 in range(GTC):
                            bank = 4 + t4
                            fns = []
                            for j in range(4):
                                k = kq * 4 + j
                                fns.append(lambda q, k=k, j=j, buf=buf, bank=bank,
                                           t4=t4:
                                           q.matmul(out=pb[bank][:],
                                                    lhsT=onT[:, k * 512 + t4 * 128:
                                                            k * 512 + (t4 + 1) * 128],
                                                    rhs=WT[:, buf * 2048 + j * 512:
                                                           buf * 2048 + (j + 1) * 512],
                                                    start=(k == 0), stop=(k == 15)))
                            kb.tgroup(fns, reads=[RONT, RWT[buf]],
                                      writes=[RPB[bank]])
                    for t4 in range(GTC):
                        t = g * GTC + t4
                        bank = 4 + t4
                        xsl = X[:, t * D + os * 512:t * D + (os + 1) * 512]
                        ro = roa[:, 0:512]
                        vop(lambda q, bank=bank, ro=ro, xsl=xsl: q.tensor_tensor(
                            out=ro, in0=pb[bank][:], in1=xsl, op=ALU.add),
                            reads=[RPB[bank], RX[t]], writes=[RRo[0]])
                        aop(lambda q, ro=ro, xsl=xsl: q.copy(
                            out=xsl, in_=ro),
                            reads=[RRo[0]], writes=[RX[t]])
                for t4 in range(GTC):
                    t = g * GTC + t4
                    xc = X[:, t * D:(t + 1) * D]
                    vop(lambda q, xc=xc, t=t: q.tensor_reduce(
                        out=st_sum[:, t:t + 1], in_=xc, axis=AX.X, op=ALU.add),
                        reads=[RX[t]], writes=[RST])
                    vop(lambda q, xc=xc, t=t: q.tensor_reduce(
                        out=st_mx[:, t:t + 1], in_=xc, axis=AX.X, op=ALU.max),
                        reads=[RX[t]], writes=[RST])
                    vop(lambda q, xc=xc, t=t: q.tensor_reduce(
                        out=st_mn[:, t:t + 1], in_=xc, axis=AX.X, op=ALU.min),
                        reads=[RX[t]], writes=[RST])
                    for sl in range(4):
                        aop(lambda q, xc=xc, sl=sl: q.activation(
                            out=oB[:], in_=xc[:, sl * 512:(sl + 1) * 512],
                            func=AF.Square, accum_out=tmpc[:, 24 + sl:25 + sl]),
                            reads=[RX[t]], writes=[ROB, RTMP])
                    vop(lambda q, t=t: q.tensor_reduce(
                        out=st_sq[:, t:t + 1], in_=tmpc[:, 24:28],
                        axis=AX.X, op=ALU.add),
                        reads=[RTMP], writes=[RST])
            A_.pop()

        if STAGE >= 1:
            emit_ffn(0)
            kb.barrier()
        if STAGE >= 2:
            emit_attn()
            kb.barrier()
        if STAGE >= 3:
            emit_ffn(1)
        else:
            A_.push()
            ost = A_.alloc(128, 2 * D, F32)
            RO = [Res("o0"), Res("o1")]
            for t in range(NTC):
                sl = ost[:, (t % 2) * D:(t % 2 + 1) * D]
                vop(lambda q, sl=sl, t=t: q.tensor_copy(
                    out=sl, in_=X[:, t * D:(t + 1) * D]),
                    reads=[RX[t]], writes=[RO[t % 2]])
                kb.dma("sync", out_t[t * 128:(t + 1) * 128, :], sl,
                       reads=[RO[t % 2]], writes=[RO[t % 2]], lane=t % 2)
            A_.pop()
        kb.barrier()

        kb.finalize(block)
    return nc


_NC_CACHE = {}


def kernel(**inputs):
    x = np.ascontiguousarray(inputs["L0_attn_x"])
    key = STAGE
    if key not in _NC_CACHE:
        _NC_CACHE[key] = build_nc()
    nc = _NC_CACHE[key]

    eyef = np.eye(128, dtype=np.float32)
    eyeb = np.eye(128, dtype=ml_dtypes.bfloat16)
    onesr = np.ones((1, 128), np.float32)
    onesb = np.ones((1, 128), ml_dtypes.bfloat16)
    onesc = np.ones((128, 128), np.float32)
    idx = np.arange(128, dtype=np.float32)
    difft = idx[None, :] - idx[:, None]
    difft = np.where(difft >= 0, difft, 1e9).astype(np.float32)
    pos1 = (idx + 1).reshape(128, 1).astype(np.float32)
    posk = (127 - idx).reshape(128, 1).astype(np.float32)

    base = dict(
        log_decay=np.ascontiguousarray(inputs["L0_attn_log_decay"]),
        Wq=np.ascontiguousarray(inputs["L0_attn_Wq"]),
        Wk=np.ascontiguousarray(inputs["L0_attn_Wk"]),
        Wv=np.ascontiguousarray(inputs["L0_attn_Wv"]),
        Wo=np.ascontiguousarray(inputs["L0_attn_Wo"]),
        Wg1=np.ascontiguousarray(inputs["L0_attn_Wg1"]),
        Wg2=np.ascontiguousarray(inputs["L0_attn_Wg2"]),
        norm_w=np.ascontiguousarray(inputs["L0_attn_norm_w"]),
        w1=np.ascontiguousarray(inputs["L0_ffn_w1"]),
        b1=np.ascontiguousarray(inputs["L0_ffn_b1"]),
        ln_g=np.ascontiguousarray(inputs["L0_ffn_ln_g"]),
        ln_b=np.ascontiguousarray(inputs["L0_ffn_ln_b"]),
        w2=np.ascontiguousarray(inputs["L0_ffn_w2"]),
        b2=np.ascontiguousarray(inputs["L0_ffn_b2"]),
        eyef=eyef, eyeb=eyeb, onesr=onesr, onesb=onesb, onesc=onesc,
        difft=difft, pos1=pos1, posk=posk,
    )
    in_maps = []
    for c in range(8):
        bb = c // 2
        hh = c % 2
        m = dict(base)
        m["x"] = np.ascontiguousarray(x[bb, hh * TPC:(hh + 1) * TPC, :])
        m["mask"] = np.array([[float(hh)]], np.float32)
        in_maps.append(m)

    res = run_bass_kernel_spmd(nc, in_maps, core_ids=list(range(8)))
    out = np.empty((B, N, D), np.float32)
    for c in range(8):
        bb = c // 2
        hh = c % 2
        out[bb, hh * TPC:(hh + 1) * TPC, :] = res.results[c]["out"]
    return out
